# version 24
# baseline (speedup 1.0000x reference)
import numpy as np
import ml_dtypes
import concourse.bass as bass
import concourse.mybir as mybir
from concourse.bass_utils import run_bass_kernel_spmd

F32 = mybir.dt.float32
BF16 = mybir.dt.bfloat16
I32 = mybir.dt.int32
AF = mybir.ActivationFunctionType
ALU = mybir.AluOpType

D = 1024
FF = 4096
NCH = D // 128
NFC = FF // 128
EPS = 1e-5

CFG = dict(TPC=4096, T1=512, T2=512, T3=512, NCORES=8)


class Plan:

    def __init__(self, sems=None):
        self.m = {}
        self.c = {}
        self.real = False
        self.sems = sems or {}

    def reset(self, real):
        self.c = {}
        self.real = real
        self._auto = 0

    def inc(self, inst, sem, ms=None, k=1):
        if ms == "@":
            ms = f"_a{self._auto}"
            self._auto += 1
        do_inc = (ms is not None) or (k == 16)
        if not do_inc:
            return self.c.get(sem, 0)
        v = self.c.get(sem, 0) + k
        self.c[sem] = v
        if ms is not None and not self.real:
            assert ms not in self.m, ms
            self.m[ms] = (sem, v)
        if self.real and inst is not None:
            inst.then_inc(self.sems[sem], k)
        return v

    def op(self, sem, ms, f, k=1):
        inst = f() if self.real else None
        return self.inc(inst, sem, ms, k)

    def dma(self, eng, sem, ms, f, gate=True):
        if gate:
            self.selfwait(eng, sem)
        return self.op(sem, ms, f, 16)

    def wait(self, eng, sem_ignored, ms):
        if self.real:
            sem, v = self.m[ms]
            if v > 0:
                eng.wait_ge(self.sems[sem], v)

    def wmil(self, eng, ms):
        self.wait(eng, None, ms)

    def wait_val(self, eng, sem, val):
        if self.real and val > 0:
            eng.wait_ge(self.sems[sem], val)

    def selfwait(self, eng, sem):
        self.wait_val(eng, sem, self.c.get(sem, 0))


def build_nc(cfg):
    TPC, T1, T2, T3 = cfg["TPC"], cfg["T1"], cfg["T2"], cfg["T3"]
    NT1, NT2, NT3 = TPC // T1, TPC // T2, TPC // T3
    NS1 = T1 // 128
    ncores = cfg["NCORES"]
    groups = [list(range(ncores // 2)), list(range(ncores // 2, ncores))]

    nc = bass.Bass("TRN2", target_bir_lowering=False)

    xt = nc.dram_tensor("xt", [D, TPC], F32, kind="ExternalInput").ap()
    wqt = nc.dram_tensor("wqt", [D, D], F32, kind="ExternalInput").ap()
    wkt = nc.dram_tensor("wkt", [D, D], F32, kind="ExternalInput").ap()
    wvt = nc.dram_tensor("wvt", [D, D], F32, kind="ExternalInput").ap()
    wot = nc.dram_tensor("wot", [D, D], F32, kind="ExternalInput").ap()
    fc1q = nc.dram_tensor("fc1q", [D, FF], I32, kind="ExternalInput").ap()
    fc2q = nc.dram_tensor("fc2q", [FF, D], I32, kind="ExternalInput").ap()
    fc1s = nc.dram_tensor("fc1s", [NCH, FF], F32, kind="ExternalInput").ap()
    fc2s = nc.dram_tensor("fc2s", [NFC, D], F32, kind="ExternalInput").ap()
    bkrow = nc.dram_tensor("bkrow", [1, D], F32, kind="ExternalInput").ap()
    bvrow = nc.dram_tensor("bvrow", [1, D], F32, kind="ExternalInput").ap()
    colpack = nc.dram_tensor("colpack", [128, 73], F32, kind="ExternalInput").ap()
    b1c_in = nc.dram_tensor("b1c", [128, 32], F32, kind="ExternalInput").ap()
    onesf = nc.dram_tensor("onesf", [1, 128], F32, kind="ExternalInput").ap()
    onesb = nc.dram_tensor("onesb", [1, 128], BF16, kind="ExternalInput").ap()
    onescb = nc.dram_tensor("onescb", [128, 1], BF16, kind="ExternalInput").ap()
    onesmin = nc.dram_tensor("onesmin", [128, 128], BF16, kind="ExternalInput").ap()
    onesmf_in = nc.dram_tensor("onesmf", [128, 128], F32, kind="ExternalInput").ap()
    e2in = nc.dram_tensor("e2in", [16, NCH * 128], F32, kind="ExternalInput").ap()
    out = nc.dram_tensor("out", [D, TPC], F32, kind="ExternalOutput").ap()

    n1d = nc.dram_tensor("n1d", [D, TPC], BF16).ap()
    hd = nc.dram_tensor("hd", [D, TPC], BF16).ap()
    w1d = nc.dram_tensor("w1d", [NCH, 128, FF], BF16).ap()
    w2d = nc.dram_tensor("w2d", [NFC, 128, D], BF16).ap()
    ccin = nc.dram_tensor("ccin", [128, 1032], F32)
    ccout = nc.dram_tensor("ccout", [128, 1032], F32)

    xt3 = xt.rearrange("(c p) t -> p c t", p=128)
    out3 = out.rearrange("(c p) t -> p c t", p=128)
    n1d3 = n1d.rearrange("(c p) t -> p c t", p=128)
    hd3 = hd.rearrange("(c p) t -> p c t", p=128)
    wqt3 = wqt.rearrange("(c p) m -> p c m", p=128)
    wkt3 = wkt.rearrange("(c p) m -> p c m", p=128)
    wvt3 = wvt.rearrange("(c p) m -> p c m", p=128)
    wot3 = wot.rearrange("(c p) m -> p c m", p=128)
    fc1q3 = fc1q.rearrange("(c p) n -> c p n", p=128)
    fc2q3 = fc2q.rearrange("(c p) n -> c p n", p=128)
    e2_3 = e2in.rearrange("h (c p) -> h c p", p=128)

    P = Plan()


    def emit_p1(M, A):
        DQU = 2 * NFC
        UPT = DQU // NT1
        assert UPT * NT1 == DQU

        def dq_qsrc(u):
            if u < 4 * NCH:
                c, qq = u // 4, u % 4
                return fc1q3[c][:, 1024 * qq:1024 * (qq + 1)]
            return fc2q3[u - 4 * NCH]

        def dq_ssrc(u):
            if u < 4 * NCH:
                c, qq = u // 4, u % 4
                return bass.AP(fc1s.tensor, c * FF + 1024 * qq, [[0, 128], [1, 1024]])
            return bass.AP(fc2s.tensor, (u - 4 * NCH) * D, [[0, 128], [1, 1024]])

        def dq_dst(u):
            if u < 4 * NCH:
                c, qq = u // 4, u % 4
                return w1d[c][:, 1024 * qq:1024 * (qq + 1)]
            return w2d[u - 4 * NCH]

        def units(i):
            return list(range(UPT * i, UPT * (i + 1)))

        def stt_chunks(i):
            us = units(i)
            base, rem = UPT // NS1, UPT % NS1
            out, p = [], 0
            for s in range(NS1):
                n = base + (1 if s < rem else 0)
                out.append(us[p:p + n])
                p += n
            return out

        def prev_pair(i, s, lag):
            g = i * NS1 + s - lag
            return (g // NS1, g % NS1) if g >= 0 else None

        def body_gp(g):
            P.op("dp0", None, lambda: g.dma_start(out=M["wq"][:, :, :], in_=wqt3), 16)
            P.op("dp0", None, lambda: g.dma_start(out=M["wo"][:, :, :], in_=wot3), 16)
            P.op("dp0", None, lambda: g.dma_start(out=A["wk"][:, :, :], in_=wkt3), 16)
            P.op("dp0", None, lambda: g.dma_start(out=A["wv"][:, :, :], in_=wvt3), 16)
            P.op("dp0", None, lambda: g.dma_start(out=M["bkb"][:, :], in_=bkrow), 16)
            P.op("dp0", None, lambda: g.dma_start(out=M["bvb"][:, :], in_=bvrow), 16)
            P.op("dp0", None, lambda: g.dma_start(out=M["cols"][:, :], in_=colpack), 16)
            P.op("dp0", None, lambda: g.dma_start(out=M["b1col"][:, :], in_=b1c_in), 16)
            P.op("dp0", None, lambda: g.dma_start(out=M["onesf"][:, :], in_=onesf), 16)
            P.op("dp0", None, lambda: g.dma_start(out=M["onesb"][:, :], in_=onesb), 16)
            P.op("dp0", None, lambda: g.dma_start(out=M["onescb"][:, :], in_=onescb), 16)
            P.op("dp0", None, lambda: g.dma_start(out=M["onesm"][:, :], in_=onesmin), 16)
            P.op("dp0", None, lambda: g.dma_start(out=M["onesmf"][:, :], in_=onesmf_in), 16)
            P.op("dp0", "p0", lambda: g.dma_start(out=M["e2"][:, :, :], in_=e2_3), 16)
            for u in (0, 1):
                P.dma(g, "ds" if u % 2 == 0 else "ds2", f"5qs_{u}",
                      lambda u=u: g.dma_start(out=A["sbc"][:, u % 2, :], in_=dq_ssrc(u)))
            for i in range(NT1):
                P.wait(g, "s_dve", f"1n1_{i}")
                P.dma(g, "dn1s", f"1n1st_{i}",
                      lambda ib=i % 2, sl=slice(i * T1, (i + 1) * T1):
                      g.dma_start(out=n1d3[:, :, sl], in_=M["n1s"][:, ib, :, 0:T1]))
                for u in units(i):
                    ub = u % 2
                    P.wmil(g, f"5w_{u}")
                    P.dma(g, "dws" if ub == 0 else "dws2", f"5ws_{u}",
                          lambda u=u, ub=ub: g.dma_start(out=dq_dst(u), in_=A["wch"][:, ub, :]))
                    if u + 2 < DQU:
                        P.dma(g, "ds" if u % 2 == 0 else "ds2", f"5qs_{u+2}",
                              lambda u=u: g.dma_start(out=A["sbc"][:, u % 2, :], in_=dq_ssrc(u + 2)))
            P.wait(g, "s_dve", "5kvev")
            P.dma(g, "dcc", None, lambda: g.dma_start(out=ccin.ap(), in_=M["ccs"][:, :]))

        def body_sync(sy):
            def emit_x(j):
                if j >= 2:
                    P.wait(sy, "s_dve", f"1t0_{j-2}")
                    P.wait(sy, "s_pe", f"1s1_{j-2}")
                    P.wait(sy, "s_act", f"1sq_{j-2}")
                P.dma(sy, "dx", f"1x_{j}",
                      lambda ib=j % 2, sl=slice(j * T1, (j + 1) * T1):
                      sy.dma_start(out=M["xts"][:, ib, :, 0:T1], in_=xt3[:, :, sl]))

            emit_x(0)
            if NT1 > 1:
                emit_x(1)
            for i in range(NT1):
                for u in units(i):
                    if u >= 4:
                        P.wmil(sy, f"5w_{u-4}")
                    P.dma(sy, ["dq", "dq2", "dq3", "dq4"][u % 4], f"5qa_{u}",
                          lambda u=u: sy.dma_start(out=A["qbf"][:, u % 4, :], in_=dq_qsrc(u)))
                if i + 2 < NT1:
                    emit_x(i + 2)
            for j in (0, 1):
                pi = NT1 - 2 + j
                P.wait(sy, "s_dve", f"1t0_{pi}")
                P.wait(sy, "s_pe", f"1s1_{pi}")
                P.wait(sy, "s_act", f"1sq_{pi}")
                P.wait(sy, "s_pe", f"1v_{pi}_{NS1-1}_1")
                P.wmil(sy, f"1n1st_{pi}")
                P.dma(sy, "dx", f"2xx_{j}",
                      lambda j=j, sl=slice(j * T2, (j + 1) * T2):
                      sy.dma_start(out=M["xts"][:, j % 2, :, 0:T2], in_=xt3[:, :, sl]))
                P.dma(sy, "dn", f"2x_{j}",
                      lambda j=j, sl=slice(j * T2, (j + 1) * T2):
                      sy.dma_start(out=M["n1s"][:, j % 2, :, 0:T2], in_=n1d3[:, :, sl]))

        def pe_stats1(t, j):
            P.wmil(t, f"1x_{j}")
            for c in range(NCH):
                P.op("s_pe", f"1s1_{j}" if c == NCH - 1 else None,
                     lambda c=c, ib=j % 2: t.matmul(A["T1p"][:, 0:T1], M["onesmf"][:, :],
                                                    M["xts"][:, ib, c, 0:T1],
                                                    start=(c == 0), stop=(c == NCH - 1), skip_group_check=True))

        def pe_stats2(t, j):
            P.wait(t, "s_act", f"1sq_{j}")
            P.wait(t, "s_act", f"1mu_{j}")
            for c in range(NCH):
                P.op("s_pe", f"1s2_{j}" if c == NCH - 1 else None,
                     lambda c=c: t.matmul(A["T1p"][:, 0:T1], M["onesm"][:, :], A["sq1"][:, c, 0:T1],
                                          start=(c == 0), stop=(c == NCH - 1), skip_group_check=True))

        def pe_kv(t, i, s):
            ib = i % 2
            tsl = slice(s * 128, (s + 1) * 128)
            pg = prev_pair(i, s, 1)
            for h in range(2):
                msl = slice(h * 512, (h + 1) * 512)
                if pg is not None:
                    P.wait(t, "s_dve", f"1mk_{pg[0]}_{pg[1]}_{h}")
                for c in range(NCH):
                    if s == 0 and h == 0:
                        P.wait(t, "s_dve", f"1n1_{i}" if c == NCH - 1 else f"1n1c_{i}_{c}")
                    P.op("s_pe", None,
                         lambda c=c, tsl=tsl, msl=msl, ib=ib:
                         t.matmul(A["kp"][:, msl], M["n1s"][:, ib, c, tsl], A["wk"][:, c, msl],
                                  start=(c == 0), stop=False, skip_group_check=True))
                    P.op("s_pe", None,
                         lambda c=c, tsl=tsl, msl=msl, ib=ib:
                         t.matmul(A["vp"][:, msl], M["n1s"][:, ib, c, tsl], A["wv"][:, c, msl],
                                  start=(c == 0), stop=False, skip_group_check=True))
                P.op("s_pe", f"1k_{i}_{s}_{h}",
                     lambda msl=msl: t.matmul(A["kp"][:, msl], M["onesb"][:, :], M["bkb"][0:1, msl],
                                              start=False, stop=True, skip_group_check=True))
                P.op("s_pe", f"1v_{i}_{s}_{h}",
                     lambda msl=msl: t.matmul(A["vp"][:, msl], M["onesb"][:, :], M["bvb"][0:1, msl],
                                              start=False, stop=True, skip_group_check=True))

        def pe_kvacc(t, i, s):
            sb = s % 2
            first = (i == 0 and s == 0)
            last = (i == NT1 - 1 and s == NS1 - 1)
            P.wait(t, "s_dve", f"1mk_{i}_{s}_1")
            for j in range(NCH):
                jsl = slice(j * 128, (j + 1) * 128)
                P.op("s_pe", None,
                     lambda j=j, jsl=jsl, sb=sb: t.matmul(A["KVp"][:, j, :], A["k_sb"][:, sb, jsl],
                                                          A["v_sb"][:, sb, jsl],
                                                          start=(first and j % 4 == 0), stop=last, skip_group_check=True))
                P.op("s_pe", f"1kv_{i}_{s}" if j == NCH - 1 else None,
                     lambda j=j, jsl=jsl, sb=sb: t.matmul(A["ksump"][:, j:j + 1], A["k_sb"][:, sb, jsl],
                                                          M["onescb"][:, :],
                                                          start=(first and j == 0), stop=last, skip_group_check=True))

        def body_pe(t):
            P.wmil(t, "p0")
            pe_stats1(t, 0)
            pe_stats2(t, 0)
            for i in range(NT1):
                P.wait(t, "s_act", f"1rstd_{i}")
                P.op("s_pe", f"1rsbc_{i}",
                     lambda: t.matmul(A["T1p"][:, 0:T1], M["onesf"][:, :], M["rows"][0:1, 3, 0:T1],
                                      start=True, stop=True, skip_group_check=True))
                if i >= 1:
                    pe_kvacc(t, i - 1, NS1 - 1)
                pe_kv(t, i, 0)
                if i + 1 < NT1:
                    P.wait(t, "s_dve", f"1n1_{i}")
                    pe_stats1(t, i + 1)
                    pe_stats2(t, i + 1)
                for s in range(1, NS1):
                    pe_kv(t, i, s)
                    pe_kvacc(t, i, s - 1)
            pe_kvacc(t, NT1 - 1, NS1 - 1)

        def act_fm(a, i, s):
            sb = s % 2
            P.wait(a, "s_dve", f"1m_{i}_{s}")
            P.op("s_act", None, lambda sb=sb: a.activation(A["e"][:, sb, 0:512], A["e"][:, sb, 0:512], AF.Exp))
            P.op("s_act", f"1e_{i}_{s}",
                 lambda sb=sb: a.activation(A["e"][:, sb, 512:1024], A["e"][:, sb, 512:1024], AF.Exp))

        def body_act(a):
            P.wmil(a, "1x_0")
            P.op("s_act", "1sq_0", lambda: a.activation(A["sq1"][:, :, 0:T1], M["xts"][:, 0, :, 0:T1], AF.Square))
            P.wait(a, "s_pe", "1s1_0")
            P.op("s_act", "1mu_0", lambda: a.copy(A["mub"][:, 0:T1], A["T1p"][:, 0:T1]))
            P.wait(a, "s_dve", "1rowA_0")
            P.op("s_act", "@", lambda: a.activation(M["rows"][0:1, 2, 0:T1], M["rows"][0:1, 1, 0:T1], AF.Ln))
            P.selfwait(a, "s_act")
            P.op("s_act", "1rstd_0",
                 lambda: a.activation(M["rows"][0:1, 3, 0:T1], M["rows"][0:1, 2, 0:T1], AF.Exp, scale=-0.5))
            for i in range(NT1):
                if i >= 1:
                    act_fm(a, i - 1, NS1 - 1)
                if i + 1 < NT1:
                    P.wmil(a, f"1x_{i+1}")
                    P.wait(a, "s_pe", f"1s2_{i}")
                    P.op("s_act", f"1sq_{i+1}",
                         lambda ib=(i + 1) % 2: a.activation(A["sq1"][:, :, 0:T1], M["xts"][:, ib, :, 0:T1], AF.Square))
                if NS1 >= 2:
                    act_fm(a, i, 0)
                if i + 1 < NT1:
                    P.wait(a, "s_pe", f"1s1_{i+1}")
                    P.wait(a, "s_dve", f"1rowA_{i}")
                    P.op("s_act", f"1mu_{i+1}", lambda: a.copy(A["mub"][:, 0:T1], A["T1p"][:, 0:T1]))
                    P.wait(a, "s_dve", f"1rowA_{i+1}")
                    P.wait(a, "s_pe", f"1rsbc_{i}")
                    P.op("s_act", "@", lambda: a.activation(M["rows"][0:1, 2, 0:T1], M["rows"][0:1, 1, 0:T1], AF.Ln))
                    P.selfwait(a, "s_act")
                    P.op("s_act", f"1rstd_{i+1}",
                         lambda: a.activation(M["rows"][0:1, 3, 0:T1], M["rows"][0:1, 2, 0:T1], AF.Exp, scale=-0.5))
                for s in range(1, NS1 - 1):
                    act_fm(a, i, s)
            act_fm(a, NT1 - 1, NS1 - 1)

        def dve_fm(v, i, s):
            sb = s % 2
            pg = prev_pair(i, s, 2)
            if pg is not None:
                P.wait(v, "s_pe", f"1kv_{pg[0]}_{pg[1]}")
            P.wait(v, "s_pe", f"1k_{i}_{s}_0")
            P.op("s_dve", None, lambda sb=sb: v.tensor_scalar(A["e"][:, sb, 0:512], A["kp"][:, 0:512], 0.0, None, ALU.min))
            P.wait(v, "s_pe", f"1k_{i}_{s}_1")
            P.op("s_dve", f"1m_{i}_{s}",
                 lambda sb=sb: v.tensor_scalar(A["e"][:, sb, 512:1024], A["kp"][:, 512:1024], 0.0, None, ALU.min))
            P.wait(v, "s_pe", f"1v_{i}_{s}_0")
            P.op("s_dve", None, lambda sb=sb: v.tensor_copy(A["v_sb"][:, sb, 0:512], A["vp"][:, 0:512]))
            P.wait(v, "s_pe", f"1v_{i}_{s}_1")
            P.op("s_dve", f"1vc_{i}_{s}", lambda sb=sb: v.tensor_copy(A["v_sb"][:, sb, 512:1024], A["vp"][:, 512:1024]))
            P.wait(v, "s_act", f"1e_{i}_{s}")
            P.op("s_dve", f"1mk_{i}_{s}_0",
                 lambda sb=sb: v.scalar_tensor_tensor(A["k_sb"][:, sb, 0:512], A["kp"][:, 0:512], 1.0,
                                                      A["e"][:, sb, 0:512], ALU.add, ALU.max))
            P.op("s_dve", f"1mk_{i}_{s}_1",
                 lambda sb=sb: v.scalar_tensor_tensor(A["k_sb"][:, sb, 512:1024], A["kp"][:, 512:1024], 1.0,
                                                      A["e"][:, sb, 512:1024], ALU.add, ALU.max))

        def dve_stt(v, u):
            P.wmil(v, f"5qa_{u}")
            P.wmil(v, f"5qs_{u}")
            if u >= 2:
                P.wmil(v, f"5ws_{u-2}")
            P.op("s_dve", f"5w_{u}",
                 lambda u=u: v.scalar_tensor_tensor(A["wch"][:, u % 2, :], A["qbf"][:, u % 4, :], -128.0,
                                                    A["sbc"][:, u % 2, :], ALU.add, ALU.mult))

        def dve_t0(v, j):
            P.wait(v, "s_act", f"1mu_{j}")
            for c in range(NCH):
                P.op("s_dve", f"1t0_{j}" if c == NCH - 1 else None,
                     lambda ib=j % 2, c=c: v.tensor_tensor(M["t0"][:, c, 0:T1], M["xts"][:, ib, c, 0:T1],
                                                           A["mub"][:, 0:T1], ALU.subtract))

        def dve_rowmath(v, j):
            P.wait(v, "s_pe", f"1s2_{j}")
            if j >= 1:
                P.wait(v, "s_act", f"1rstd_{j-1}")
            P.op("s_dve", "@", lambda: v.tensor_tensor(M["rows"][0:1, 2, 0:T1], A["mub"][0:1, 0:T1], A["mub"][0:1, 0:T1], ALU.mult))
            P.selfwait(v, "s_dve")
            P.op("s_dve", "@", lambda: v.tensor_tensor(M["rows"][0:1, 1, 0:T1], A["T1p"][0:1, 0:T1], M["rows"][0:1, 2, 0:T1], ALU.subtract))
            P.selfwait(v, "s_dve")
            P.op("s_dve", f"1rowA_{j}", lambda: v.tensor_scalar(M["rows"][0:1, 1, 0:T1], M["rows"][0:1, 1, 0:T1], EPS, None, ALU.add))

        def body_dve(v):
            dve_t0(v, 0)
            dve_rowmath(v, 0)
            for i in range(NT1):
                if i >= 1:
                    dve_fm(v, i - 1, NS1 - 1)
                P.wait(v, "s_pe", f"1rsbc_{i}")
                if i >= 2:
                    P.wait(v, "s_pe", f"1v_{i-2}_{NS1-1}_1")
                    P.wmil(v, f"1n1st_{i-2}")
                for c in range(NCH):
                    P.op("s_dve", f"1n1_{i}" if c == NCH - 1 else f"1n1c_{i}_{c}",
                         lambda ib=i % 2, c=c: v.tensor_tensor(M["n1s"][:, ib, c, 0:T1], M["t0"][:, c, 0:T1],
                                                               A["T1p"][:, 0:T1], ALU.mult))
                ch = stt_chunks(i)
                for u in ch[0]:
                    dve_stt(v, u)
                for s in range(NS1 - 1):
                    dve_fm(v, i, s)
                    if s == 0 and i + 1 < NT1:
                        dve_t0(v, i + 1)
                        dve_rowmath(v, i + 1)
                    for u in ch[s + 1]:
                        dve_stt(v, u)
                if NS1 == 1 and i + 1 < NT1:
                    dve_t0(v, i + 1)
                    dve_rowmath(v, i + 1)
            dve_fm(v, NT1 - 1, NS1 - 1)
            P.wait(v, "s_pe", f"1kv_{NT1-1}_{NS1-1}")
            ccs_kv1 = M["ccs"][:, 0:1024].rearrange("p (c m) -> p c m", m=128) if P.real else None
            P.op("s_dve", None, lambda: v.tensor_copy(ccs_kv1, A["KVp"][:, :, :]))
            P.op("s_dve", "5kvev", lambda: v.tensor_copy(M["ccs"][:, 1024:1032], A["ksump"][:, :]))

        if P.real:
            with nc.Block("p1") as blk:
                blk.gpsimd(body_gp)
                blk.sync(body_sync)
                blk.scalar(body_act)
                blk.tensor(body_pe)
                blk.vector(body_dve)
        else:
            body_gp(None); body_sync(None); body_act(None); body_pe(None); body_dve(None)

    def emit_p2a(M, C, A15):
        ccs_kv = None
        if P.real:
            ccs_kv = M["ccs"][:, 0:1024].rearrange("p (c m) -> p c m", m=128)

        def body_sync(sy):
            for i in range(2, NT2):
                ib = i % 2
                sl = slice(i * T2, (i + 1) * T2)
                P.wait(sy, "s_dve", f"2x1_{i-2}_{NCH-1}")
                P.wait(sy, "s_pe", f"2qp_{i-2}_{NCH-1}")
                P.wmil(sy, f"1n1st_{i}")
                P.dma(sy, "dx", f"2xx_{i}", lambda ib=ib, sl=sl: sy.dma_start(out=M["xts"][:, ib, :, 0:T2], in_=xt3[:, :, sl]))
                P.dma(sy, "dn", f"2x_{i}", lambda ib=ib, sl=sl: sy.dma_start(out=M["n1s"][:, ib, :, 0:T2], in_=n1d3[:, :, sl]))
            P.wmil(sy, "5ws_62")
            P.wmil(sy, "5ws_63")
            w1r_a = w1d.rearrange("c p n -> p c n")
            w1sems_a = ["dw1", "dw2", "dq", "dq2", "dq3", "dq4", "ds3", "ds4"]
            for q in range(NCH):
                if q == 0:
                    P.wait(sy, "s_pe", f"2qp_{NT2-1}_{NCH-1}")
                elif q == 2:
                    P.wait(sy, "s_pe", f"2wo_{NT2-1}_{NCH-1}")
                elif q == 4:
                    P.wait(sy, "s_dve", f"2x1_{NT2-1}_{NCH-1}")
                P.dma(sy, w1sems_a[q], f"3w1_{q}",
                      lambda q=q: sy.dma_start(out=w1b_v[:, q, :], in_=w1r_a[:, q, :]))

        def body_act(a):
            P.wmil(a, "5ws_62")
            P.wmil(a, "5ws_63")

            def emit_eq(it):
                for j in range(NCH):
                    jb = j % 3
                    P.wait(a, "s_dve", f"2qm_{it}_{j}")
                    P.op("s_act", f"2eq_{it}_{j}",
                         lambda jb=jb: a.activation(C["e_q"][:, jb, 0:T2], C["e_q"][:, jb, 0:T2], AF.Exp))

            def ln2_tail(j):
                P.wait(a, "s_pe", f"2s1_{j}_{NCH-1}")
                if j >= 1:
                    P.wait(a, "s_pl", f"2t_{j-1}")
                P.op("s_act", f"2bce1_{j}", lambda: a.copy(C["bcmu"][:, 0:T2], C["T1b"][:, 0:T2]))
                for c in range(NCH):
                    cb2 = c % 2
                    P.wait(a, "s_dve", f"2x1_{j}_{c}")
                    if c >= 2:
                        P.wait(a, "s_pe", f"2s2_{j}_{c-2}")
                    elif j >= 1:
                        P.wait(a, "s_pe", f"2s2_{j-1}_{c + NCH - 2}")
                    P.op("s_act", f"2sqc_{j}_{c}",
                         lambda c=c, cb2=cb2: a.activation(C["sq2"][:, cb2, 0:T2], C["x1"][:, c, 0:T2], AF.Square))
                P.wait(a, "s_dve", f"2rowA_{j}")
                P.op("s_act", "@", lambda: a.activation(M["rows"][0:1, 2, 0:T2], M["rows"][0:1, 4, 0:T2], AF.Ln))
                P.selfwait(a, "s_act")
                P.op("s_act", f"2rstd1_{j}", lambda: a.activation(M["rows"][0:1, 3, 0:T2], M["rows"][0:1, 2, 0:T2], AF.Exp, scale=-0.5))
                P.selfwait(a, "s_act")
                P.op("s_act", f"2rw_{j}", lambda: a.activation(M["rows"][0:1, 2, 0:T2], M["rows"][0:1, 2, 0:T2], AF.Exp, scale=-1.0))
                P.wait(a, "s_dve", f"2rowB_{j}")
                P.op("s_act", "@", lambda: a.activation(M["rows"][0:1, 4, 0:T2], M["rows"][0:1, 1, 0:T2], AF.Ln))
                P.selfwait(a, "s_act")
                P.op("s_act", f"2rstd2_{j}", lambda: a.activation(M["rows"][0:1, 4, 0:T2], M["rows"][0:1, 4, 0:T2], AF.Exp, scale=-0.5))

            def bce_tail(j):
                P.wait(a, "s_pe", f"2bc2_{j}")
                if j >= 1:
                    P.wait(a, "s_pl", f"2n2_{j-1}_{NCH-1}")
                P.op("s_act", f"2bce2_{j}", lambda: a.copy(C["bcrs"][:, 0:T2], C["T1b"][:, 0:T2]))
                P.wait(a, "s_pe", f"2bc3_{j}")
                if j >= 1:
                    P.wait(a, "s_pl", f"2sc2_{j-1}_{NCH-1}")
                P.op("s_act", f"2bce3_{j}", lambda: a.copy(C["bcr12"][:, 0:T2], C["T1b"][:, 0:T2]))

            def h_tail(j):
                for c in range(NCH):
                    P.wait(a, "s_pl", f"2sc2_{j}_{c}")
                    if c == 0 and j >= 1:
                        P.wmil(a, f"2hst_{j-1}")
                    P.op("s_act", f"2h_{j}_{c}",
                         lambda c=c: a.activation(C["h_sb"][:, c, 0:T2], C["scr2"][:, c % 2, 0:T2], AF.Identity,
                                                  bias=M["cols"][:, 24 + c:25 + c], scale=M["cols"][:, 16 + c:17 + c]))

            emit_eq(0)
            for i in range(NT2):
                P.wait(a, "s_pe", f"2zp_{i}")
                P.op("s_act", "@", lambda: a.activation(C["zr"][:, 0:T2], C["B0"][0:16, 0:T2], AF.Ln, bias=M["cols"][0:16, 72:73]))
                P.selfwait(a, "s_act")
                P.op("s_act", f"2z_{i}", lambda: a.activation(C["zr"][:, 0:T2], C["zr"][:, 0:T2], AF.Exp, scale=-1.0))
                for c in range(NCH):
                    cb = c % 2
                    P.wait(a, "s_pe", f"2zb_{i}_{c}")
                    if c >= 2:
                        P.wait(a, "s_dve", f"2at_{i}_{c-2}")
                    elif i >= 1:
                        P.wait(a, "s_dve", f"2at_{i-1}_{c + NCH - 2}")
                    P.op("s_act", f"2zbe_{i}_{c}",
                         lambda cb=cb: a.copy(C["zbc"][:, cb, 0:T2], C[f"B{cb}"][:, 0:T2]))
                if i >= 1:
                    ln2_tail(i - 1)
                if i + 1 < NT2:
                    emit_eq(i + 1)
                if i >= 1:
                    bce_tail(i - 1)
                    h_tail(i - 1)
            ln2_tail(NT2 - 1)
            bce_tail(NT2 - 1)
            h_tail(NT2 - 1)

        def body_pe(t):
            def emit_q(it):
                P.wmil(t, f"2xx_{it}")
                P.wmil(t, f"2x_{it}")
                for j in range(NCH):
                    jb = j % 3
                    if j >= 3:
                        P.wait(t, "s_dve", f"2qe_{it}_{j-3}")
                    elif it >= 1:
                        P.wait(t, "s_dve", f"2qe_{it-1}_{NCH-3+j}")
                    for c in range(NCH):
                        P.op("s_pe", f"2qp_{it}_{j}" if c == NCH - 1 else None,
                             lambda j=j, jb=jb, c=c, ib=it % 2: t.matmul(C[f"G{jb}"][:, 0:T2], M["wq"][:, c, 128 * j:128 * (j + 1)], M["n1s"][:, ib, c, 0:T2],
                                                                         start=(c == 0), stop=(c == NCH - 1), skip_group_check=True))

            def ln2_stats(j):
                if j >= 1:
                    P.wait(t, "s_act", f"2bce3_{j-1}")
                for c in range(NCH):
                    cb2 = c % 2
                    P.wait(t, "s_pl", f"2xc_{j}_{c}")
                    P.op("s_pe", f"2s1_{j}_{c}",
                         lambda c=c, cb2=cb2: t.matmul(C["T1b"][:, 0:T2], M["onesm"][:, :], C["x1b"][:, cb2, 0:T2],
                                              start=(c == 0), stop=(c == NCH - 1), skip_group_check=True))
                P.wait(t, "s_act", f"2bce1_{j}")
                for c in range(NCH):
                    cb2 = c % 2
                    P.wait(t, "s_act", f"2sqc_{j}_{c}")
                    P.op("s_pe", f"2s2_{j}_{c}",
                         lambda c=c, cb2=cb2: t.matmul(C["T1b"][:, 0:T2], M["onesm"][:, :], C["sq2"][:, cb2, 0:T2],
                                                       start=(c == 0), stop=(c == NCH - 1), skip_group_check=True))

            def ln2_bc(j):
                P.wait(t, "s_act", f"2rstd1_{j}")
                P.op("s_pe", f"2bc2_{j}", lambda: t.matmul(C["T1b"][:, 0:T2], M["onesf"][:, :], M["rows"][0:1, 3, 0:T2],
                                                           start=True, stop=True, skip_group_check=True))
                P.wait(t, "s_act", f"2bce2_{j}")
                P.wait(t, "s_dve", f"2r12_{j}")
                P.op("s_pe", f"2bc3_{j}", lambda: t.matmul(C["T1b"][:, 0:T2], M["onesf"][:, :], M["rows"][0:1, 0, 0:T2],
                                                           start=True, stop=True, skip_group_check=True))

            emit_q(0)
            P.wait(t, "s_dve", "5done")
            for i in range(NT2):
                P.wait(t, "s_dve", f"2qe_{i}_{NCH-1}")
                if i >= 1:
                    P.wait(t, "s_act", f"2zbe_{i-1}_{NCH-2}")
                for c in range(NCH):
                    P.op("s_pe", f"2zp_{i}" if c == NCH - 1 else None,
                         lambda c=c: t.matmul(C["B0"][:, 0:T2], M["kmask"][:, c, :], C["q_b"][:, c, 0:T2],
                                              start=(c == 0), stop=(c == NCH - 1), skip_group_check=True))
                P.wait(t, "s_act", f"2z_{i}")
                for c in range(NCH):
                    cb = c % 2
                    if c >= 2:
                        P.wait(t, "s_act", f"2zbe_{i}_{c-2}")
                    elif i >= 1:
                        P.wait(t, "s_act", f"2zbe_{i-1}_{c + NCH - 2}")
                    P.op("s_pe", f"2zb_{i}_{c}",
                         lambda c=c, cb=cb: t.matmul(C[f"B{cb}"][:, 0:T2], M["e2"][:, c, :], C["zr"][:, 0:T2],
                                                     start=True, stop=True, skip_group_check=True))
                    if c >= 2:
                        P.wait(t, "s_dve", f"2at_{i}_{c-2}")
                    elif i >= 1:
                        P.wait(t, "s_dve", f"2x1_{i-1}_{NCH-2+c}")
                    P.op("s_pe", f"2ap_{i}_{c}",
                         lambda c=c, cb=cb: t.matmul(C[f"A{cb}"][:, 0:T2], M["kvt"][:, c, :], C["q_b"][:, c, 0:T2],
                                                     start=True, stop=True, skip_group_check=True))
                if i >= 1:
                    ln2_stats(i - 1)
                if i + 1 < NT2:
                    emit_q(i + 1)
                if i >= 1:
                    ln2_bc(i - 1)
                P.wait(t, "s_dve", f"2at_{i}_{NCH-1}")
                for k in range(NCH):
                    kb = k % 2
                    if k >= 2:
                        P.wait(t, "s_dve", f"2x1_{i}_{k-2}")
                    else:
                        P.wait(t, "s_dve", f"2at_{i}_{NCH-2+k}")
                    for c in range(NCH):
                        P.op("s_pe", f"2wo_{i}_{k}" if c == NCH - 1 else None,
                             lambda k=k, kb=kb, c=c: t.matmul(C[f"A{kb}"][:, 0:T2], M["wo"][:, c, 128 * k:128 * (k + 1)], C["at_b"][:, c, 0:T2],
                                                              start=(c == 0), stop=(c == NCH - 1), skip_group_check=True))
            ln2_stats(NT2 - 1)
            ln2_bc(NT2 - 1)

        def body_dve(v):
            P.wmil(v, "5ws_62")
            P.wmil(v, "5ws_63")
            P.op("s_dve", "@", lambda: v.tensor_scalar(M["dcol"][:, 0:8], M["cols"][:, 0:8], 1.0, None, ALU.add))
            P.op("s_dve", "5dcol", lambda: v.tensor_scalar(M["dcol"][:, 8:16], M["cols"][:, 32:40], 0.5, None, ALU.mult))
            P.wmil(v, "5ccl")
            P.op("s_dve", "@", lambda: v.memset(M["kvt"][:, :, :], 0.0))
            P.op("s_dve", "@", lambda: v.memset(M["kmask"][:, :, :], 0.0))
            P.selfwait(v, "s_dve")
            for j in range(NCH):
                P.op("s_dve", None, lambda j=j: v.tensor_copy(M["kvt"][0:64, j, 0:64], ccs_kv[0:64, j, 0:64]))
                P.op("s_dve", None, lambda j=j: v.tensor_copy(M["kvt"][64:128, j, 64:128], ccs_kv[64:128, j, 64:128]))
                P.op("s_dve", None, lambda j=j: v.tensor_copy(M["kmask"][0:64, j, 2 * j:2 * j + 1], M["ccs"][0:64, 1024 + j:1025 + j]))
                ms = "5done" if j == NCH - 1 else None
                P.op("s_dve", ms, lambda j=j: v.tensor_copy(M["kmask"][64:128, j, 2 * j + 1:2 * j + 2], M["ccs"][64:128, 1024 + j:1025 + j]))

            def emit_qmqe(it):
                def emit_qm(j):
                    jb = j % 3
                    P.wait(v, "s_pe", f"2qp_{it}_{j}")
                    P.op("s_dve", f"2qm_{it}_{j}",
                         lambda j=j, jb=jb: v.tensor_scalar(C["e_q"][:, jb, 0:T2], C[f"G{jb}"][:, 0:T2], M["cols"][:, j:j + 1], 0.0, ALU.add, ALU.min))

                def emit_qe(j):
                    jb = j % 3
                    P.wait(v, "s_act", f"2eq_{it}_{j}")
                    P.op("s_dve", f"2qe_{it}_{j}",
                         lambda j=j, jb=jb: v.scalar_tensor_tensor(C["q_b"][:, j, 0:T2], C[f"G{jb}"][:, 0:T2], M["dcol"][:, j:j + 1], C["e_q"][:, jb, 0:T2], ALU.add, ALU.max))

                if it >= 1:
                    P.wait(v, "s_pe", f"2zp_{it-1}")
                    P.wait(v, "s_pe", f"2ap_{it-1}_{NCH-1}")
                emit_qm(0)
                for j in range(1, NCH):
                    emit_qm(j)
                    emit_qe(j - 1)
                emit_qe(NCH - 1)

            def rowmath(j):
                if j >= 1:
                    P.wait(v, "s_pe", f"2bc3_{j-1}")
                P.wait(v, "s_act", f"2bce1_{j}")
                P.op("s_dve", "@", lambda: v.tensor_tensor(M["rows"][0:1, 2, 0:T2], C["bcmu"][0:1, 0:T2], C["bcmu"][0:1, 0:T2], ALU.mult))
                P.wait(v, "s_pe", f"2s2_{j}_{NCH-1}")
                P.selfwait(v, "s_dve")
                P.op("s_dve", "@", lambda: v.tensor_tensor(M["rows"][0:1, 1, 0:T2], C["T1b"][0:1, 0:T2], M["rows"][0:1, 2, 0:T2], ALU.subtract))
                P.selfwait(v, "s_dve")
                P.op("s_dve", f"2rowA_{j}", lambda: v.tensor_scalar(M["rows"][0:1, 4, 0:T2], M["rows"][0:1, 1, 0:T2], EPS, None, ALU.add))
                P.wait(v, "s_act", f"2rw_{j}")
                P.op("s_dve", "@", lambda: v.tensor_tensor(M["rows"][0:1, 1, 0:T2], M["rows"][0:1, 1, 0:T2], M["rows"][0:1, 2, 0:T2], ALU.mult))
                P.selfwait(v, "s_dve")
                P.op("s_dve", f"2rowB_{j}", lambda: v.tensor_scalar(M["rows"][0:1, 1, 0:T2], M["rows"][0:1, 1, 0:T2], EPS, None, ALU.add))
                P.wait(v, "s_act", f"2rstd2_{j}")
                P.op("s_dve", f"2r12_{j}", lambda: v.tensor_tensor(M["rows"][0:1, 0, 0:T2], M["rows"][0:1, 3, 0:T2], M["rows"][0:1, 4, 0:T2], ALU.mult))

            def cprime_k(j, k):
                P.wait(v, "s_pl", f"2n2_{j}_{k}")
                if k == 0 and j >= 1:
                    P.wmil(v, f"2cst_{j-1}")
                P.op("s_dve", f"2c_{j}_{k}",
                     lambda k=k: v.scalar_tensor_tensor(C["cp"][:, k, 0:T2], C["scr1"][:, k % 2, 0:T2], M["dcol"][:, 8 + k:9 + k], C["x1"][:, k, 0:T2], ALU.add, ALU.add))

            emit_qmqe(0)
            for i in range(NT2):
                ib = i % 2
                for c in range(NCH):
                    cb = c % 2
                    P.wait(v, "s_pe", f"2ap_{i}_{c}")
                    P.wait(v, "s_act", f"2zbe_{i}_{c}")
                    P.op("s_dve", f"2at_{i}_{c}",
                         lambda c=c, cb=cb: v.tensor_tensor(C["at_b"][:, c, 0:T2], C[f"A{cb}"][:, 0:T2], C["zbc"][:, cb, 0:T2], ALU.mult))
                if i >= 1:
                    rowmath(i - 1)
                if i + 1 < NT2:
                    emit_qmqe(i + 1)
                for k in range(NCH):
                    kb = k % 2
                    if i >= 1:
                        cprime_k(i - 1, k)
                        P.selfwait(v, "s_dve")
                    P.wait(v, "s_pe", f"2wo_{i}_{k}")
                    if k == 0 and i >= 1:
                        P.wait(v, "s_act", f"2sqc_{i-1}_{NCH-1}")
                        P.wait(v, "s_pe", f"2s1_{i-1}_{NCH-1}")
                        P.wait(v, "s_pl", f"2t_{i-1}")
                    P.op("s_dve", f"2x1_{i}_{k}",
                         lambda k=k, kb=kb, ib=ib: v.scalar_tensor_tensor(C["x1"][:, k, 0:T2], C[f"A{kb}"][:, 0:T2], M["cols"][:, 8 + k:9 + k], M["xts"][:, ib, k, 0:T2], ALU.add, ALU.add))
            rowmath(NT2 - 1)
            for k in range(NCH):
                cprime_k(NT2 - 1, k)

        def body_gp(g):
            P.selfwait(g, "dcc")
            P.op("s_cc", "5cc", lambda: g.collective_compute(
                "AllReduce", ALU.add, replica_groups=groups,
                ins=[ccin.ap().opt()], outs=[ccout.ap().opt()]), 1)
            P.wait(g, "s_cc", "5cc")
            P.dma(g, "dcc", "5ccl", lambda: g.dma_start(out=M["ccs"][:, :], in_=ccout.ap()))

            for i in range(NT2):
                sl = slice(i * T2, (i + 1) * T2)
                for c in range(NCH):
                    cb2 = c % 2
                    P.wait(g, "s_dve", f"2x1_{i}_{c}")
                    if c >= 2:
                        P.wait(g, "s_pe", f"2s1_{i}_{c-2}")
                    elif i >= 1:
                        P.wait(g, "s_pe", f"2s1_{i-1}_{c + NCH - 2}")
                    P.op("s_pl", f"2xc_{i}_{c}",
                         lambda c=c, cb2=cb2: g.tensor_copy(C["x1b"][:, cb2, 0:T2], C["x1"][:, c, 0:T2]))
                P.wait(g, "s_act", f"2bce1_{i}")
                P.wait(g, "s_dve", f"2x1_{i}_{NCH-1}")
                for c in range(NCH):
                    P.op("s_pl", f"2t_{i}" if c == NCH - 1 else "@",
                         lambda c=c: g.tensor_tensor(M["t0"][:, c, 0:T2], C["x1"][:, c, 0:T2], C["bcmu"][:, 0:T2], ALU.subtract))
                P.selfwait(g, "s_pl")
                P.wait(g, "s_act", f"2bce3_{i}")
                for k in range(NCH):
                    if k >= 2:
                        P.wait(g, "s_act", f"2h_{i}_{k-2}")
                    elif i >= 1:
                        P.wait(g, "s_act", f"2h_{i-1}_{NCH-2+k}")
                    P.op("s_pl", f"2sc2_{i}_{k}",
                         lambda k=k: g.tensor_tensor(C["scr2"][:, k % 2, 0:T2], M["t0"][:, k, 0:T2], C["bcr12"][:, 0:T2], ALU.mult))
                    if k >= 2:
                        P.wait(g, "s_dve", f"2c_{i}_{k-2}")
                    P.op("s_pl", f"2n2_{i}_{k}",
                         lambda k=k: g.tensor_tensor(C["scr1"][:, k % 2, 0:T2], M["t0"][:, k, 0:T2], C["bcrs"][:, 0:T2], ALU.mult))
                P.wait(g, "s_act", f"2h_{i}_{NCH-1}")
                P.dma(g, "dhs", f"2hst_{i}", lambda sl=sl: g.dma_start(out=hd3[:, :, sl], in_=C["h_sb"][:, :, 0:T2]))
                P.wait(g, "s_dve", f"2c_{i}_{NCH-1}")
                P.dma(g, "dcs", f"2cst_{i}", lambda sl=sl: g.dma_start(out=out3[:, :, sl], in_=C["cp"][:, :, 0:T2]))

        if P.real:
            with nc.Block("p2a") as blk:
                blk.sync(body_sync)
                blk.scalar(body_act)
                blk.tensor(body_pe)
                blk.vector(body_dve)
                blk.gpsimd(body_gp)
        else:
            body_sync(None); body_act(None); body_pe(None); body_dve(None); body_gp(None)


    def emit_p2b(Dd):
        def body_sync(sy):
            P.wmil(sy, "5ws_62")
            P.wmil(sy, "5ws_63")
            P.wmil(sy, f"2hst_{NT2-1}")
            P.wmil(sy, f"2cst_{NT2-1}")
            for i in (0, 1):
                P.dma(sy, "dh", f"3h_{i}",
                      lambda ib=i % 2, sl=slice(i * T3, (i + 1) * T3):
                      sy.dma_start(out=Dd["h_t"][:, ib, :, 0:T3], in_=hd3[:, :, sl]))
            w2r = w2d.rearrange("c p n -> p c n")
            w2sems = ["dn", "dx", "dq", "dq2"]
            for q in range(4):
                P.dma(sy, w2sems[q], f"3w2_{q}",
                      lambda q=q: sy.dma_start(out=Dd["w2b"][:, :, 256 * q:256 * (q + 1)],
                                               in_=w2r[:, :, 256 * q:256 * (q + 1)]))
            for i in range(2, NT3):
                ib = i % 2
                sl = slice(i * T3, (i + 1) * T3)
                P.wait(sy, "s_pe", f"3y_{i-2}_{NFC-1}")
                P.dma(sy, "dh", f"3h_{i}", lambda ib=ib, sl=sl: sy.dma_start(out=Dd["h_t"][:, ib, :, 0:T3], in_=hd3[:, :, sl]))

        def body_pe(t):
            for i in range(NT3):
                ib = i % 2
                P.wmil(t, f"3h_{i}")
                for j in range(NFC):
                    jb = j % 2
                    if i == 0 and j == 0:
                        for q in range(NCH):
                            P.wmil(t, f"3w1_{q}")
                    if j >= 2:
                        P.wait(t, "s_act", f"3sl_{i}_{j-2}")
                    elif i >= 1:
                        P.wait(t, "s_act", f"3sl_{i-1}_{NFC-2+j}")
                    for c in range(NCH):
                        P.op("s_pe", f"3y_{i}_{j}" if c == NCH - 1 else None,
                             lambda j=j, jb=jb, c=c, ib=ib: t.matmul(Dd[f"yp{jb}"][:, 0:T3], Dd["w1b"][:, c, 128 * j:128 * (j + 1)], Dd["h_t"][:, ib, c, 0:T3],
                                                                     start=(c == 0), stop=(c == NCH - 1), skip_group_check=True))
                for k in range(NCH):
                    kb = k % 2
                    if i == 0 and k % 2 == 0:
                        P.wmil(t, f"3w2_{k // 2}")
                    if k == 0:
                        P.wait(t, "s_act", f"3sl_{i}_{NFC-1}")
                    if k >= 2:
                        P.wait(t, "s_act", f"3z_{i}_{k-2}")
                    elif i >= 1:
                        P.wait(t, "s_act", f"3z_{i-1}_{NCH-2+k}")
                    for j in range(NFC):
                        P.op("s_pe", f"3f2_{i}_{k}" if j == NFC - 1 else None,
                             lambda j=j, k=k, kb=kb: t.matmul(Dd[f"zp{kb}"][:, 0:T3], Dd["w2b"][:, j, 128 * k:128 * (k + 1)], Dd["y_sb"][:, j, 0:T3],
                                                              start=(j == 0), stop=(j == NFC - 1), skip_group_check=True))

        def body_act(a):
            P.wmil(a, f"2hst_{NT2-1}")
            P.wmil(a, f"2cst_{NT2-1}")
            for i in range(NT3):
                ib = i % 2
                for j in range(NFC):
                    jb = j % 2
                    P.wait(a, "s_pe", f"3y_{i}_{j}")
                    if j == 0 and i >= 1:
                        P.wait(a, "s_pe", f"3f2_{i-1}_{NCH-1}")
                    P.op("s_act", f"3sl_{i}_{j}",
                         lambda j=j, jb=jb: a.activation(Dd["y_sb"][:, j, 0:T3], Dd[f"yp{jb}"][:, 0:T3], AF.Silu, bias=M0["b1col"][:, j:j + 1]))
                for k in range(NCH):
                    kb = k % 2
                    hb = 0 if k < 4 else 1
                    P.wait(a, "s_pe", f"3f2_{i}_{k}")
                    if i >= 1 and k == 0:
                        P.wmil(a, f"3ostA_{i-1}")
                    if i >= 1 and k == 4:
                        P.wmil(a, f"3ostB_{i-1}")
                    P.op("s_act", f"3z_{i}_{k}",
                         lambda k=k, kb=kb, hb=hb: a.mul(Dd["outst"][:, hb, k % 4, 0:T3], Dd[f"zp{kb}"][:, 0:T3], 0.5))

        def body_gp(g):
            P.wmil(g, f"2cst_{NT2-1}")
            for i in range(NT3):
                ib = i % 2
                sl = slice(i * T3, (i + 1) * T3)
                P.wait(g, "s_act", f"3z_{i}_{3}")
                P.dma(g, "dos", f"3ostA_{i}",
                      lambda sl=sl: g.dma_start(out=out3[:, 0:4, sl], in_=Dd["outst"][:, 0, :, 0:T3], accum_op=ALU.add))
                P.wait(g, "s_act", f"3z_{i}_{NCH-1}")
                P.dma(g, "dos", f"3ostB_{i}",
                      lambda sl=sl: g.dma_start(out=out3[:, 4:8, sl], in_=Dd["outst"][:, 1, :, 0:T3], accum_op=ALU.add))
            P.selfwait(g, "dos")
            P.selfwait(g, "dcs")
            P.selfwait(g, "dhs")

        if P.real:
            with nc.Block("p2b") as blk:
                blk.sync(body_sync)
                blk.tensor(body_pe)
                blk.scalar(body_act)
                blk.gpsimd(body_gp)
        else:
            body_sync(None); body_pe(None); body_act(None); body_gp(None)

    from contextlib import ExitStack
    M0 = {}

    def sb(es, name, shape, dt):
        return es.enter_context(nc.sbuf_tensor(name, shape, dt))

    def ps(es, name, shape):
        return es.enter_context(nc.psum_tensor(name, shape, F32))

    with ExitStack() as es0:
        P.sems = {n: es0.enter_context(nc.semaphore(n))
                  for n in ["dp0", "dx", "dn", "dh", "dw1", "dw2", "dn1s",
                            "dq", "dq2", "dq3", "dq4", "ds", "ds2", "ds3", "ds4", "dcc", "dws", "dws2", "dhs", "dcs", "dos",
                            "s_cc", "s_pe", "s_dve", "s_act", "s_pl"]}
        b1col_t = sb(es0, "b1col", [128, 32], F32)
        M0["b1col"] = b1col_t

        P.reset(False)
        Mn = {"b1col": None}
        emit_p1(Mn, None)
        emit_p2a(Mn, None, None)
        emit_p2b(None)

        P.reset(True)
        TT = max(T1, T2)
        with ExitStack() as esA:
            arena = sb(esA, "arena", [128, 32768], BF16)
            arena_f = arena.bitcast(F32)
            wq_v = arena.ap()[:, 0:NCH * D].rearrange("p (c n) -> p c n", n=D)
            wo_v = arena.ap()[:, NCH * D:2 * NCH * D].rearrange("p (c n) -> p c n", n=D)
            xts_v = arena_f.ap()[:, NCH * D:NCH * D + 2 * NCH * TT].rearrange(
                "p (b c t) -> p b c t", b=2, c=NCH)
            w1b_v = arena.ap().rearrange("p (c n) -> p c n", n=FF)
            assert NCH * D + 2 * NCH * TT <= 16384
            es1 = esA.enter_context(ExitStack())
            M = dict(
                cols=sb(es1, "cols", [128, 73], F32),
                dcol=sb(es1, "dcol", [128, 16], F32),
                kvt=sb(es1, "kvt", [128, NCH, 128], BF16),
                kmask=sb(es1, "kmask", [128, NCH, 128], BF16),
                e2=sb(es1, "e2", [16, NCH, 128], F32),
                ccs=sb(es1, "ccs", [128, 1032], F32),
                onesf=sb(es1, "onesf_s", [1, 128], F32),
                onesb=sb(es1, "onesb_s", [1, 128], BF16),
                onescb=sb(es1, "onescb_s", [128, 1], BF16),
                onesm=sb(es1, "onesm_s", [128, 128], BF16),
                onesmf=sb(es1, "onesmf_s", [128, 128], F32),
                bkb=sb(es1, "bkb", [1, D], BF16),
                bvb=sb(es1, "bvb", [1, D], BF16),
                rows=sb(es1, "rows", [1, 5, max(T1, T2)], F32),
                wq=wq_v,
                wo=wo_v,
                xts=xts_v,
                n1s=sb(es1, "n1s", [128, 2, NCH, TT], BF16),
                t0=sb(es1, "t0", [128, NCH, TT], F32),
                b1col=b1col_t,
            )
            with ExitStack() as es2o:
                kv_ps = dict(
                    KVp=ps(es2o, "KVp", [128, NCH, 128]),
                    ksump=ps(es2o, "ksump", [128, NCH]),
                )
                with ExitStack() as es2:
                    A = dict(
                        wk=sb(es2, "wk", [128, NCH, D], BF16),
                        wv=sb(es2, "wv", [128, NCH, D], BF16),
                        sq1=sb(es2, "sq1", [128, NCH, T1], BF16),
                        k_sb=sb(es2, "k_sb", [128, 2, D], BF16),
                        v_sb=sb(es2, "v_sb", [128, 2, D], BF16),
                        e=sb(es2, "e", [128, 2, D], BF16),
                        kp=ps(es2, "kp", [128, 1024]),
                        vp=ps(es2, "vp", [128, 1024]),
                        T1p=ps(es2, "T1p", [128, 512]),
                        mub=sb(es2, "mub", [128, T1], F32),
                        qbf=sb(es2, "qbf", [128, 4, 1024], I32),
                        sbc=sb(es2, "sbc", [128, 2, 1024], F32),
                        wch=sb(es2, "wch", [128, 2, 1024], BF16),
                        **kv_ps,
                    )
                    emit_p1(M, A)
            with ExitStack() as es4:
                C = dict(
                    e_q=sb(es4, "e_q", [128, 3, T2], BF16),
                    x1b=sb(es4, "x1b", [128, 2, T2], BF16),
                    q_b=sb(es4, "q_b", [128, NCH, T2], BF16),
                    at_b=sb(es4, "at_b", [128, NCH, T2], BF16),
                    zbc=sb(es4, "zbc", [128, 2, T2], F32),
                    zr=sb(es4, "zr", [16, T2], F32),
                    x1=sb(es4, "x1", [128, NCH, T2], F32),
                    sq2=sb(es4, "sq2", [128, 2, T2], BF16),
                    bcmu=sb(es4, "bcmu", [128, T2], F32),
                    bcrs=sb(es4, "bcrs", [128, T2], F32),
                    bcr12=sb(es4, "bcr12", [128, T2], F32),
                    scr1=sb(es4, "scr1", [128, 2, T2], F32),
                    scr2=sb(es4, "scr2", [128, 2, T2], F32),
                    h_sb=sb(es4, "h_sb", [128, NCH, T2], BF16),
                    cp=sb(es4, "cp", [128, NCH, T2], F32),
                    B0=ps(es4, "B0", [128, T2]),
                    B1=ps(es4, "B1", [128, T2]),
                    T1b=ps(es4, "T1b", [128, T2]),
                    G0=ps(es4, "G0", [128, T2]),
                    G1=ps(es4, "G1", [128, T2]),
                    G2=ps(es4, "G2", [128, T2]),
                    A0=ps(es4, "A0", [128, T2]),
                    A1=ps(es4, "A1", [128, T2]),
                )
                emit_p2a(M, C, kv_ps)
            es1.close()
            es5 = esA.enter_context(ExitStack())
            Dd = dict(
                w1b=w1b_v,
                w2b=sb(es5, "w2b", [128, NFC, D], BF16),
                h_t=sb(es5, "h_t", [128, 2, NCH, T3], BF16),
                y_sb=sb(es5, "y_sb", [128, NFC, T3], BF16),
                outst=sb(es5, "outst", [128, 2, 4, T3], F32),
                yp0=ps(es5, "yp0", [128, T3]),
                yp1=ps(es5, "yp1", [128, T3]),
                zp0=ps(es5, "zp0", [128, T3]),
                zp1=ps(es5, "zp1", [128, T3]),
            )
            emit_p2b(Dd)

    return nc


_NC_CACHE = {}


def _get_nc():
    key = tuple(sorted(CFG.items()))
    if key not in _NC_CACHE:
        _NC_CACHE[key] = build_nc(CFG)
    return _NC_CACHE[key]


def prep_inputs(inputs, cfg):
    TPC = cfg["TPC"]
    ncores = cfg["NCORES"]
    x = np.asarray(inputs["L0_attn_hidden_states"], dtype=np.float32)
    ntok = x.shape[0] * x.shape[1]
    assert ntok == TPC * ncores
    xall = np.ascontiguousarray(x.reshape(ntok, D).T)

    def T(a):
        return np.ascontiguousarray(np.asarray(a, dtype=np.float32).T)

    def colform(a, nch):
        return np.ascontiguousarray(np.asarray(a, dtype=np.float32).reshape(nch, 128).T)

    wqt = T(inputs["L0_attn_Wq"])
    wkt = T(inputs["L0_attn_Wk"])
    wvt = T(inputs["L0_attn_Wv"])
    wot = T(inputs["L0_attn_Wo"])
    fc1q = np.ascontiguousarray(np.asarray(inputs["L0_ffn_fc1_q"], dtype=np.int32))
    fc2q = np.ascontiguousarray(np.asarray(inputs["L0_ffn_fc2_q"], dtype=np.int32))
    fc1s = np.ascontiguousarray(np.asarray(inputs["L0_ffn_fc1_scales"], dtype=np.float32))
    fc2s = np.ascontiguousarray(np.asarray(inputs["L0_ffn_fc2_scales"], dtype=np.float32))
    bkrow = np.asarray(inputs["L0_attn_bk"], dtype=np.float32).reshape(1, D).copy()
    bvrow = np.asarray(inputs["L0_attn_bv"], dtype=np.float32).reshape(1, D).copy()
    colpack = np.zeros((128, 73), np.float32)
    colpack[:, 72] = 1e-6
    colpack[:, 0:8] = colform(inputs["L0_attn_bq"], NCH)
    colpack[:, 8:16] = colform(inputs["L0_attn_bo"], NCH)
    colpack[:, 16:24] = colform(inputs["L0_ffn_ln_gamma"], NCH)
    colpack[:, 24:32] = colform(inputs["L0_ffn_ln_beta"], NCH)
    colpack[:, 32:40] = colform(inputs["L0_ffn_fc2_bias"], NCH)
    b1c = colform(inputs["L0_ffn_fc1_bias"], NFC)
    onesf = np.ones((1, 128), np.float32)
    onesmin = np.full((128, 128), 1.0 / 1024.0, ml_dtypes.bfloat16)
    onesmf = np.full((128, 128), 1.0 / 1024.0, np.float32)
    onesb = np.ones((1, 128), ml_dtypes.bfloat16)
    onescb = np.ones((128, 1), ml_dtypes.bfloat16)
    e2 = np.zeros((16, NCH, 128), np.float32)
    for c in range(NCH):
        for a in range(2):
            e2[2 * c + a, c, 64 * a:64 * (a + 1)] = 1.0
    e2 = e2.reshape(16, NCH * 128)

    shared = dict(wqt=wqt, wkt=wkt, wvt=wvt, wot=wot, fc1q=fc1q, fc2q=fc2q,
                  fc1s=fc1s, fc2s=fc2s, bkrow=bkrow, bvrow=bvrow,
                  colpack=colpack, b1c=b1c, onesf=onesf, onesb=onesb,
                  onescb=onescb, onesmin=onesmin, onesmf=onesmf, e2in=e2)
    in_maps = []
    for c in range(ncores):
        m = dict(shared)
        m["xt"] = np.ascontiguousarray(xall[:, c * TPC:(c + 1) * TPC])
        in_maps.append(m)
    return in_maps


def assemble_output(results, x_shape, cfg):
    TPC = cfg["TPC"]
    ncores = cfg["NCORES"]
    cols = [np.asarray(results[c]["out"], dtype=np.float32) for c in range(ncores)]
    full = np.concatenate(cols, axis=1)
    return np.ascontiguousarray(full.T).reshape(x_shape)


def kernel(**inputs):
    cfg = CFG
    nc = _get_nc()
    in_maps = prep_inputs(inputs, cfg)
    res = run_bass_kernel_spmd(nc, in_maps, core_ids=list(range(cfg["NCORES"])))
    x_shape = np.asarray(inputs["L0_attn_hidden_states"]).shape
    return assemble_output(res.results, x_shape, cfg)


# version 25
# speedup vs baseline: 1.0087x; 1.0087x over previous
import numpy as np
import ml_dtypes
import concourse.bass as bass
import concourse.mybir as mybir
from concourse.bass_utils import run_bass_kernel_spmd

F32 = mybir.dt.float32
BF16 = mybir.dt.bfloat16
I32 = mybir.dt.int32
AF = mybir.ActivationFunctionType
ALU = mybir.AluOpType

D = 1024
FF = 4096
NCH = D // 128
NFC = FF // 128
EPS = 1e-5

CFG = dict(TPC=4096, T1=512, T2=512, T3=512, NCORES=8)


class Plan:

    def __init__(self, sems=None):
        self.m = {}
        self.c = {}
        self.real = False
        self.sems = sems or {}

    def reset(self, real):
        self.c = {}
        self.real = real
        self._auto = 0

    def inc(self, inst, sem, ms=None, k=1):
        if ms == "@":
            ms = f"_a{self._auto}"
            self._auto += 1
        do_inc = (ms is not None) or (k == 16)
        if not do_inc:
            return self.c.get(sem, 0)
        v = self.c.get(sem, 0) + k
        self.c[sem] = v
        if ms is not None and not self.real:
            assert ms not in self.m, ms
            self.m[ms] = (sem, v)
        if self.real and inst is not None:
            inst.then_inc(self.sems[sem], k)
        return v

    def op(self, sem, ms, f, k=1):
        inst = f() if self.real else None
        return self.inc(inst, sem, ms, k)

    def dma(self, eng, sem, ms, f, gate=True):
        if gate:
            self.selfwait(eng, sem)
        return self.op(sem, ms, f, 16)

    def wait(self, eng, sem_ignored, ms):
        if self.real:
            sem, v = self.m[ms]
            if v > 0:
                eng.wait_ge(self.sems[sem], v)

    def wmil(self, eng, ms):
        self.wait(eng, None, ms)

    def wait_val(self, eng, sem, val):
        if self.real and val > 0:
            eng.wait_ge(self.sems[sem], val)

    def selfwait(self, eng, sem):
        self.wait_val(eng, sem, self.c.get(sem, 0))


def build_nc(cfg):
    TPC, T1, T2, T3 = cfg["TPC"], cfg["T1"], cfg["T2"], cfg["T3"]
    NT1, NT2, NT3 = TPC // T1, TPC // T2, TPC // T3
    NS1 = T1 // 128
    ncores = cfg["NCORES"]
    groups = [list(range(ncores // 2)), list(range(ncores // 2, ncores))]

    nc = bass.Bass("TRN2", target_bir_lowering=False)

    xt = nc.dram_tensor("xt", [D, TPC], F32, kind="ExternalInput").ap()
    wqt = nc.dram_tensor("wqt", [D, D], F32, kind="ExternalInput").ap()
    wkt = nc.dram_tensor("wkt", [D, D], F32, kind="ExternalInput").ap()
    wvt = nc.dram_tensor("wvt", [D, D], F32, kind="ExternalInput").ap()
    wot = nc.dram_tensor("wot", [D, D], F32, kind="ExternalInput").ap()
    fc1q = nc.dram_tensor("fc1q", [D, FF], I32, kind="ExternalInput").ap()
    fc2q = nc.dram_tensor("fc2q", [FF, D], I32, kind="ExternalInput").ap()
    fc1s = nc.dram_tensor("fc1s", [NCH, FF], F32, kind="ExternalInput").ap()
    fc2s = nc.dram_tensor("fc2s", [NFC, D], F32, kind="ExternalInput").ap()
    bkrow = nc.dram_tensor("bkrow", [1, D], F32, kind="ExternalInput").ap()
    bvrow = nc.dram_tensor("bvrow", [1, D], F32, kind="ExternalInput").ap()
    colpack = nc.dram_tensor("colpack", [128, 73], F32, kind="ExternalInput").ap()
    b1c_in = nc.dram_tensor("b1c", [128, 32], F32, kind="ExternalInput").ap()
    onesf = nc.dram_tensor("onesf", [1, 128], F32, kind="ExternalInput").ap()
    onesb = nc.dram_tensor("onesb", [1, 128], BF16, kind="ExternalInput").ap()
    onescb = nc.dram_tensor("onescb", [128, 1], BF16, kind="ExternalInput").ap()
    onesmin = nc.dram_tensor("onesmin", [128, 128], BF16, kind="ExternalInput").ap()
    onesmf_in = nc.dram_tensor("onesmf", [128, 128], F32, kind="ExternalInput").ap()
    e2in = nc.dram_tensor("e2in", [16, NCH * 128], F32, kind="ExternalInput").ap()
    out = nc.dram_tensor("out", [D, TPC], F32, kind="ExternalOutput").ap()

    n1d = nc.dram_tensor("n1d", [D, TPC], BF16).ap()
    hd = nc.dram_tensor("hd", [D, TPC], BF16).ap()
    w1d = nc.dram_tensor("w1d", [NCH, 128, FF], BF16).ap()
    w2d = nc.dram_tensor("w2d", [NFC, 128, D], BF16).ap()
    ccin = nc.dram_tensor("ccin", [128, 1032], F32)
    ccout = nc.dram_tensor("ccout", [128, 1032], F32)

    xt3 = xt.rearrange("(c p) t -> p c t", p=128)
    out3 = out.rearrange("(c p) t -> p c t", p=128)
    n1d3 = n1d.rearrange("(c p) t -> p c t", p=128)
    hd3 = hd.rearrange("(c p) t -> p c t", p=128)
    wqt3 = wqt.rearrange("(c p) m -> p c m", p=128)
    wkt3 = wkt.rearrange("(c p) m -> p c m", p=128)
    wvt3 = wvt.rearrange("(c p) m -> p c m", p=128)
    wot3 = wot.rearrange("(c p) m -> p c m", p=128)
    fc1q3 = fc1q.rearrange("(c p) n -> c p n", p=128)
    fc2q3 = fc2q.rearrange("(c p) n -> c p n", p=128)
    e2_3 = e2in.rearrange("h (c p) -> h c p", p=128)

    P = Plan()


    def emit_p1(M, A):
        DQU = 2 * NFC
        UPT = DQU // NT1
        assert UPT * NT1 == DQU

        def dq_qsrc(u):
            if u < 4 * NCH:
                c, qq = u // 4, u % 4
                return fc1q3[c][:, 1024 * qq:1024 * (qq + 1)]
            return fc2q3[u - 4 * NCH]

        def dq_ssrc(u):
            if u < 4 * NCH:
                c, qq = u // 4, u % 4
                return bass.AP(fc1s.tensor, c * FF + 1024 * qq, [[0, 128], [1, 1024]])
            return bass.AP(fc2s.tensor, (u - 4 * NCH) * D, [[0, 128], [1, 1024]])

        def dq_dst(u):
            if u < 4 * NCH:
                c, qq = u // 4, u % 4
                return w1d[c][:, 1024 * qq:1024 * (qq + 1)]
            return w2d[u - 4 * NCH]

        def units(i):
            return list(range(UPT * i, UPT * (i + 1)))

        def stt_chunks(i):
            us = units(i)
            base, rem = UPT // NS1, UPT % NS1
            out, p = [], 0
            for s in range(NS1):
                n = base + (1 if s < rem else 0)
                out.append(us[p:p + n])
                p += n
            return out

        def prev_pair(i, s, lag):
            g = i * NS1 + s - lag
            return (g // NS1, g % NS1) if g >= 0 else None

        def body_gp(g):
            P.op("dp0", None, lambda: g.dma_start(out=M["wq"][:, :, :], in_=wqt3), 16)
            P.op("dp0", None, lambda: g.dma_start(out=M["wo"][:, :, :], in_=wot3), 16)
            P.op("dp0", None, lambda: g.dma_start(out=A["wk"][:, :, :], in_=wkt3), 16)
            P.op("dp0", None, lambda: g.dma_start(out=A["wv"][:, :, :], in_=wvt3), 16)
            P.op("dp0", None, lambda: g.dma_start(out=M["bkb"][:, :], in_=bkrow), 16)
            P.op("dp0", None, lambda: g.dma_start(out=M["bvb"][:, :], in_=bvrow), 16)
            P.op("dp0", None, lambda: g.dma_start(out=M["cols"][:, :], in_=colpack), 16)
            P.op("dp0", None, lambda: g.dma_start(out=M["b1col"][:, :], in_=b1c_in), 16)
            P.op("dp0", None, lambda: g.dma_start(out=M["onesf"][:, :], in_=onesf), 16)
            P.op("dp0", None, lambda: g.dma_start(out=M["onesb"][:, :], in_=onesb), 16)
            P.op("dp0", None, lambda: g.dma_start(out=M["onescb"][:, :], in_=onescb), 16)
            P.op("dp0", None, lambda: g.dma_start(out=M["onesm"][:, :], in_=onesmin), 16)
            P.op("dp0", None, lambda: g.dma_start(out=M["onesmf"][:, :], in_=onesmf_in), 16)
            P.op("dp0", "p0", lambda: g.dma_start(out=M["e2"][:, :, :], in_=e2_3), 16)
            for u in (0, 1):
                P.dma(g, "ds" if u % 2 == 0 else "ds2", f"5qs_{u}",
                      lambda u=u: g.dma_start(out=A["sbc"][:, u % 2, :], in_=dq_ssrc(u)))
            for i in range(NT1):
                P.wait(g, "s_dve", f"1n1_{i}")
                P.dma(g, "dn1s", f"1n1st_{i}",
                      lambda ib=i % 2, sl=slice(i * T1, (i + 1) * T1):
                      g.dma_start(out=n1d3[:, :, sl], in_=M["n1s"][:, ib, :, 0:T1]))
                for u in units(i):
                    ub = u % 2
                    P.wmil(g, f"5w_{u}")
                    P.dma(g, "dws" if ub == 0 else "dws2", f"5ws_{u}",
                          lambda u=u, ub=ub: g.dma_start(out=dq_dst(u), in_=A["wch"][:, ub, :]))
                    if u + 2 < DQU:
                        P.dma(g, "ds" if u % 2 == 0 else "ds2", f"5qs_{u+2}",
                              lambda u=u: g.dma_start(out=A["sbc"][:, u % 2, :], in_=dq_ssrc(u + 2)))
            P.wait(g, "s_dve", "5kvev")
            P.dma(g, "dcc", None, lambda: g.dma_start(out=ccin.ap(), in_=M["ccs"][:, :]))
            P.selfwait(g, "dcc")
            P.op("s_cc", "5cc", lambda: g.collective_compute(
                "AllReduce", ALU.add, replica_groups=groups,
                ins=[ccin.ap().opt()], outs=[ccout.ap().opt()]), 1)

        def body_sync(sy):
            def emit_x(j):
                if j >= 2:
                    P.wait(sy, "s_dve", f"1t0_{j-2}")
                    P.wait(sy, "s_pe", f"1s1_{j-2}")
                    P.wait(sy, "s_act", f"1sq_{j-2}")
                P.dma(sy, "dx", f"1x_{j}",
                      lambda ib=j % 2, sl=slice(j * T1, (j + 1) * T1):
                      sy.dma_start(out=M["xts"][:, ib, :, 0:T1], in_=xt3[:, :, sl]))

            emit_x(0)
            if NT1 > 1:
                emit_x(1)
            for i in range(NT1):
                for u in units(i):
                    if u >= 4:
                        P.wmil(sy, f"5w_{u-4}")
                    P.dma(sy, ["dq", "dq2", "dq3", "dq4"][u % 4], f"5qa_{u}",
                          lambda u=u: sy.dma_start(out=A["qbf"][:, u % 4, :], in_=dq_qsrc(u)))
                if i + 2 < NT1:
                    emit_x(i + 2)
            for j in (0, 1):
                pi = NT1 - 2 + j
                P.wait(sy, "s_dve", f"1t0_{pi}")
                P.wait(sy, "s_pe", f"1s1_{pi}")
                P.wait(sy, "s_act", f"1sq_{pi}")
                P.wait(sy, "s_pe", f"1v_{pi}_{NS1-1}_1")
                P.wmil(sy, f"1n1st_{pi}")
                P.dma(sy, "dx", f"2xx_{j}",
                      lambda j=j, sl=slice(j * T2, (j + 1) * T2):
                      sy.dma_start(out=M["xts"][:, j % 2, :, 0:T2], in_=xt3[:, :, sl]))
                P.dma(sy, "dn", f"2x_{j}",
                      lambda j=j, sl=slice(j * T2, (j + 1) * T2):
                      sy.dma_start(out=M["n1s"][:, j % 2, :, 0:T2], in_=n1d3[:, :, sl]))

        def pe_stats1(t, j):
            P.wmil(t, f"1x_{j}")
            for c in range(NCH):
                P.op("s_pe", f"1s1_{j}" if c == NCH - 1 else None,
                     lambda c=c, ib=j % 2: t.matmul(A["T1p"][:, 0:T1], M["onesmf"][:, :],
                                                    M["xts"][:, ib, c, 0:T1],
                                                    start=(c == 0), stop=(c == NCH - 1), skip_group_check=True))

        def pe_stats2(t, j):
            P.wait(t, "s_act", f"1sq_{j}")
            P.wait(t, "s_act", f"1mu_{j}")
            for c in range(NCH):
                P.op("s_pe", f"1s2_{j}" if c == NCH - 1 else None,
                     lambda c=c: t.matmul(A["T1p"][:, 0:T1], M["onesm"][:, :], A["sq1"][:, c, 0:T1],
                                          start=(c == 0), stop=(c == NCH - 1), skip_group_check=True))

        def pe_kv(t, i, s):
            ib = i % 2
            tsl = slice(s * 128, (s + 1) * 128)
            pg = prev_pair(i, s, 1)
            for h in range(2):
                msl = slice(h * 512, (h + 1) * 512)
                if pg is not None:
                    P.wait(t, "s_dve", f"1mk_{pg[0]}_{pg[1]}_{h}")
                for c in range(NCH):
                    if s == 0 and h == 0:
                        P.wait(t, "s_dve", f"1n1_{i}" if c == NCH - 1 else f"1n1c_{i}_{c}")
                    P.op("s_pe", None,
                         lambda c=c, tsl=tsl, msl=msl, ib=ib:
                         t.matmul(A["kp"][:, msl], M["n1s"][:, ib, c, tsl], A["wk"][:, c, msl],
                                  start=(c == 0), stop=False, skip_group_check=True))
                    P.op("s_pe", None,
                         lambda c=c, tsl=tsl, msl=msl, ib=ib:
                         t.matmul(A["vp"][:, msl], M["n1s"][:, ib, c, tsl], A["wv"][:, c, msl],
                                  start=(c == 0), stop=False, skip_group_check=True))
                P.op("s_pe", f"1k_{i}_{s}_{h}",
                     lambda msl=msl: t.matmul(A["kp"][:, msl], M["onesb"][:, :], M["bkb"][0:1, msl],
                                              start=False, stop=True, skip_group_check=True))
                P.op("s_pe", f"1v_{i}_{s}_{h}",
                     lambda msl=msl: t.matmul(A["vp"][:, msl], M["onesb"][:, :], M["bvb"][0:1, msl],
                                              start=False, stop=True, skip_group_check=True))

        def pe_kvacc(t, i, s):
            sb = s % 2
            first = (i == 0 and s == 0)
            last = (i == NT1 - 1 and s == NS1 - 1)
            P.wait(t, "s_dve", f"1mk_{i}_{s}_1")
            for j in range(NCH):
                jsl = slice(j * 128, (j + 1) * 128)
                P.op("s_pe", None,
                     lambda j=j, jsl=jsl, sb=sb: t.matmul(A["KVp"][:, j, :], A["k_sb"][:, sb, jsl],
                                                          A["v_sb"][:, sb, jsl],
                                                          start=(first and j % 4 == 0), stop=last, skip_group_check=True))
                P.op("s_pe", f"1kv_{i}_{s}" if j == NCH - 1 else None,
                     lambda j=j, jsl=jsl, sb=sb: t.matmul(A["ksump"][:, j:j + 1], A["k_sb"][:, sb, jsl],
                                                          M["onescb"][:, :],
                                                          start=(first and j == 0), stop=last, skip_group_check=True))

        def body_pe(t):
            P.wmil(t, "p0")
            pe_stats1(t, 0)
            pe_stats2(t, 0)
            for i in range(NT1):
                P.wait(t, "s_act", f"1rstd_{i}")
                P.op("s_pe", f"1rsbc_{i}",
                     lambda: t.matmul(A["T1p"][:, 0:T1], M["onesf"][:, :], M["rows"][0:1, 3, 0:T1],
                                      start=True, stop=True, skip_group_check=True))
                if i >= 1:
                    pe_kvacc(t, i - 1, NS1 - 1)
                pe_kv(t, i, 0)
                if i + 1 < NT1:
                    P.wait(t, "s_dve", f"1n1_{i}")
                    pe_stats1(t, i + 1)
                    pe_stats2(t, i + 1)
                for s in range(1, NS1):
                    pe_kv(t, i, s)
                    pe_kvacc(t, i, s - 1)
            pe_kvacc(t, NT1 - 1, NS1 - 1)

        def act_fm(a, i, s):
            sb = s % 2
            P.wait(a, "s_dve", f"1m_{i}_{s}")
            P.op("s_act", None, lambda sb=sb: a.activation(A["e"][:, sb, 0:512], A["e"][:, sb, 0:512], AF.Exp))
            P.op("s_act", f"1e_{i}_{s}",
                 lambda sb=sb: a.activation(A["e"][:, sb, 512:1024], A["e"][:, sb, 512:1024], AF.Exp))

        def body_act(a):
            P.wmil(a, "1x_0")
            P.op("s_act", "1sq_0", lambda: a.activation(A["sq1"][:, :, 0:T1], M["xts"][:, 0, :, 0:T1], AF.Square))
            P.wait(a, "s_pe", "1s1_0")
            P.op("s_act", "1mu_0", lambda: a.copy(A["mub"][:, 0:T1], A["T1p"][:, 0:T1]))
            P.wait(a, "s_dve", "1rowA_0")
            P.op("s_act", "@", lambda: a.activation(M["rows"][0:1, 2, 0:T1], M["rows"][0:1, 1, 0:T1], AF.Ln))
            P.selfwait(a, "s_act")
            P.op("s_act", "1rstd_0",
                 lambda: a.activation(M["rows"][0:1, 3, 0:T1], M["rows"][0:1, 2, 0:T1], AF.Exp, scale=-0.5))
            for i in range(NT1):
                if i >= 1:
                    act_fm(a, i - 1, NS1 - 1)
                if i + 1 < NT1:
                    P.wmil(a, f"1x_{i+1}")
                    P.wait(a, "s_pe", f"1s2_{i}")
                    P.op("s_act", f"1sq_{i+1}",
                         lambda ib=(i + 1) % 2: a.activation(A["sq1"][:, :, 0:T1], M["xts"][:, ib, :, 0:T1], AF.Square))
                if NS1 >= 2:
                    act_fm(a, i, 0)
                if i + 1 < NT1:
                    P.wait(a, "s_pe", f"1s1_{i+1}")
                    P.wait(a, "s_dve", f"1rowA_{i}")
                    P.op("s_act", f"1mu_{i+1}", lambda: a.copy(A["mub"][:, 0:T1], A["T1p"][:, 0:T1]))
                    P.wait(a, "s_dve", f"1rowA_{i+1}")
                    P.wait(a, "s_pe", f"1rsbc_{i}")
                    P.op("s_act", "@", lambda: a.activation(M["rows"][0:1, 2, 0:T1], M["rows"][0:1, 1, 0:T1], AF.Ln))
                    P.selfwait(a, "s_act")
                    P.op("s_act", f"1rstd_{i+1}",
                         lambda: a.activation(M["rows"][0:1, 3, 0:T1], M["rows"][0:1, 2, 0:T1], AF.Exp, scale=-0.5))
                for s in range(1, NS1 - 1):
                    act_fm(a, i, s)
            act_fm(a, NT1 - 1, NS1 - 1)

        def dve_fm(v, i, s):
            sb = s % 2
            pg = prev_pair(i, s, 2)
            if pg is not None:
                P.wait(v, "s_pe", f"1kv_{pg[0]}_{pg[1]}")
            P.wait(v, "s_pe", f"1k_{i}_{s}_0")
            P.op("s_dve", None, lambda sb=sb: v.tensor_scalar(A["e"][:, sb, 0:512], A["kp"][:, 0:512], 0.0, None, ALU.min))
            P.wait(v, "s_pe", f"1k_{i}_{s}_1")
            P.op("s_dve", f"1m_{i}_{s}",
                 lambda sb=sb: v.tensor_scalar(A["e"][:, sb, 512:1024], A["kp"][:, 512:1024], 0.0, None, ALU.min))
            P.wait(v, "s_pe", f"1v_{i}_{s}_0")
            P.op("s_dve", None, lambda sb=sb: v.tensor_copy(A["v_sb"][:, sb, 0:512], A["vp"][:, 0:512]))
            P.wait(v, "s_pe", f"1v_{i}_{s}_1")
            P.op("s_dve", f"1vc_{i}_{s}", lambda sb=sb: v.tensor_copy(A["v_sb"][:, sb, 512:1024], A["vp"][:, 512:1024]))
            P.wait(v, "s_act", f"1e_{i}_{s}")
            P.op("s_dve", f"1mk_{i}_{s}_0",
                 lambda sb=sb: v.scalar_tensor_tensor(A["k_sb"][:, sb, 0:512], A["kp"][:, 0:512], 1.0,
                                                      A["e"][:, sb, 0:512], ALU.add, ALU.max))
            P.op("s_dve", f"1mk_{i}_{s}_1",
                 lambda sb=sb: v.scalar_tensor_tensor(A["k_sb"][:, sb, 512:1024], A["kp"][:, 512:1024], 1.0,
                                                      A["e"][:, sb, 512:1024], ALU.add, ALU.max))

        def dve_stt(v, u):
            P.wmil(v, f"5qa_{u}")
            P.wmil(v, f"5qs_{u}")
            if u >= 2:
                P.wmil(v, f"5ws_{u-2}")
            P.op("s_dve", f"5w_{u}",
                 lambda u=u: v.scalar_tensor_tensor(A["wch"][:, u % 2, :], A["qbf"][:, u % 4, :], -128.0,
                                                    A["sbc"][:, u % 2, :], ALU.add, ALU.mult))

        def dve_t0(v, j):
            P.wait(v, "s_act", f"1mu_{j}")
            for c in range(NCH):
                P.op("s_dve", f"1t0_{j}" if c == NCH - 1 else None,
                     lambda ib=j % 2, c=c: v.tensor_tensor(M["t0"][:, c, 0:T1], M["xts"][:, ib, c, 0:T1],
                                                           A["mub"][:, 0:T1], ALU.subtract))

        def dve_rowmath(v, j):
            P.wait(v, "s_pe", f"1s2_{j}")
            if j >= 1:
                P.wait(v, "s_act", f"1rstd_{j-1}")
            P.op("s_dve", "@", lambda: v.tensor_tensor(M["rows"][0:1, 2, 0:T1], A["mub"][0:1, 0:T1], A["mub"][0:1, 0:T1], ALU.mult))
            P.selfwait(v, "s_dve")
            P.op("s_dve", "@", lambda: v.tensor_tensor(M["rows"][0:1, 1, 0:T1], A["T1p"][0:1, 0:T1], M["rows"][0:1, 2, 0:T1], ALU.subtract))
            P.selfwait(v, "s_dve")
            P.op("s_dve", f"1rowA_{j}", lambda: v.tensor_scalar(M["rows"][0:1, 1, 0:T1], M["rows"][0:1, 1, 0:T1], EPS, None, ALU.add))

        def body_dve(v):
            dve_t0(v, 0)
            dve_rowmath(v, 0)
            for i in range(NT1):
                if i >= 1:
                    dve_fm(v, i - 1, NS1 - 1)
                P.wait(v, "s_pe", f"1rsbc_{i}")
                if i >= 2:
                    P.wait(v, "s_pe", f"1v_{i-2}_{NS1-1}_1")
                    P.wmil(v, f"1n1st_{i-2}")
                for c in range(NCH):
                    P.op("s_dve", f"1n1_{i}" if c == NCH - 1 else f"1n1c_{i}_{c}",
                         lambda ib=i % 2, c=c: v.tensor_tensor(M["n1s"][:, ib, c, 0:T1], M["t0"][:, c, 0:T1],
                                                               A["T1p"][:, 0:T1], ALU.mult))
                ch = stt_chunks(i)
                for u in ch[0]:
                    dve_stt(v, u)
                for s in range(NS1 - 1):
                    dve_fm(v, i, s)
                    if s == 0 and i + 1 < NT1:
                        dve_t0(v, i + 1)
                        dve_rowmath(v, i + 1)
                    for u in ch[s + 1]:
                        dve_stt(v, u)
                if NS1 == 1 and i + 1 < NT1:
                    dve_t0(v, i + 1)
                    dve_rowmath(v, i + 1)
            dve_fm(v, NT1 - 1, NS1 - 1)
            P.wait(v, "s_pe", f"1kv_{NT1-1}_{NS1-1}")
            ccs_kv1 = M["ccs"][:, 0:1024].rearrange("p (c m) -> p c m", m=128) if P.real else None
            P.op("s_dve", None, lambda: v.tensor_copy(ccs_kv1, A["KVp"][:, :, :]))
            P.op("s_dve", "5kvev", lambda: v.tensor_copy(M["ccs"][:, 1024:1032], A["ksump"][:, :]))

        if P.real:
            with nc.Block("p1") as blk:
                blk.gpsimd(body_gp)
                blk.sync(body_sync)
                blk.scalar(body_act)
                blk.tensor(body_pe)
                blk.vector(body_dve)
        else:
            body_gp(None); body_sync(None); body_act(None); body_pe(None); body_dve(None)

    def emit_p2a(M, C, A15):
        ccs_kv = None
        if P.real:
            ccs_kv = M["ccs"][:, 0:1024].rearrange("p (c m) -> p c m", m=128)

        def body_sync(sy):
            for i in range(2, NT2):
                ib = i % 2
                sl = slice(i * T2, (i + 1) * T2)
                P.wait(sy, "s_dve", f"2x1_{i-2}_{NCH-1}")
                P.wait(sy, "s_pe", f"2qp_{i-2}_{NCH-1}")
                P.wmil(sy, f"1n1st_{i}")
                P.dma(sy, "dx", f"2xx_{i}", lambda ib=ib, sl=sl: sy.dma_start(out=M["xts"][:, ib, :, 0:T2], in_=xt3[:, :, sl]))
                P.dma(sy, "dn", f"2x_{i}", lambda ib=ib, sl=sl: sy.dma_start(out=M["n1s"][:, ib, :, 0:T2], in_=n1d3[:, :, sl]))
            P.wmil(sy, "5ws_62")
            P.wmil(sy, "5ws_63")
            w1r_a = w1d.rearrange("c p n -> p c n")
            w1sems_a = ["dw1", "dw2", "dq", "dq2", "dq3", "dq4", "ds3", "ds4"]
            for q in range(NCH):
                if q == 0:
                    P.wait(sy, "s_pe", f"2qp_{NT2-1}_{NCH-1}")
                elif q == 2:
                    P.wait(sy, "s_pe", f"2wo_{NT2-1}_{NCH-1}")
                elif q == 4:
                    P.wait(sy, "s_dve", f"2x1_{NT2-1}_{NCH-1}")
                P.dma(sy, w1sems_a[q], f"3w1_{q}",
                      lambda q=q: sy.dma_start(out=w1b_v[:, q, :], in_=w1r_a[:, q, :]))

        def body_act(a):
            P.wmil(a, "5ws_62")
            P.wmil(a, "5ws_63")

            def emit_eq(it, j0=0, j1=NCH):
                for j in range(j0, j1):
                    jb = j % 3
                    P.wait(a, "s_dve", f"2qm_{it}_{j}")
                    P.op("s_act", f"2eq_{it}_{j}",
                         lambda jb=jb: a.activation(C["e_q"][:, jb, 0:T2], C["e_q"][:, jb, 0:T2], AF.Exp))

            def ln2_tail(j):
                P.wait(a, "s_pe", f"2s1_{j}_{NCH-1}")
                if j >= 1:
                    P.wait(a, "s_pl", f"2t_{j-1}")
                P.op("s_act", f"2bce1_{j}", lambda: a.copy(C["bcmu"][:, 0:T2], C["T1b"][:, 0:T2]))
                for c in range(NCH):
                    cb2 = c % 2
                    P.wait(a, "s_dve", f"2x1_{j}_{c}")
                    if c >= 2:
                        P.wait(a, "s_pe", f"2s2_{j}_{c-2}")
                    elif j >= 1:
                        P.wait(a, "s_pe", f"2s2_{j-1}_{c + NCH - 2}")
                    P.op("s_act", f"2sqc_{j}_{c}",
                         lambda c=c, cb2=cb2: a.activation(C["sq2"][:, cb2, 0:T2], C["x1"][:, c, 0:T2], AF.Square))
                P.wait(a, "s_dve", f"2rowA_{j}")
                P.op("s_act", "@", lambda: a.activation(M["rows"][0:1, 2, 0:T2], M["rows"][0:1, 4, 0:T2], AF.Ln))
                P.selfwait(a, "s_act")
                P.op("s_act", f"2rstd1_{j}", lambda: a.activation(M["rows"][0:1, 3, 0:T2], M["rows"][0:1, 2, 0:T2], AF.Exp, scale=-0.5))
                P.selfwait(a, "s_act")
                P.op("s_act", f"2rw_{j}", lambda: a.activation(M["rows"][0:1, 2, 0:T2], M["rows"][0:1, 2, 0:T2], AF.Exp, scale=-1.0))
                P.wait(a, "s_dve", f"2rowB_{j}")
                P.op("s_act", "@", lambda: a.activation(M["rows"][0:1, 4, 0:T2], M["rows"][0:1, 1, 0:T2], AF.Ln))
                P.selfwait(a, "s_act")
                P.op("s_act", f"2rstd2_{j}", lambda: a.activation(M["rows"][0:1, 4, 0:T2], M["rows"][0:1, 4, 0:T2], AF.Exp, scale=-0.5))

            def bce_tail(j):
                P.wait(a, "s_pe", f"2bc2_{j}")
                if j >= 1:
                    P.wait(a, "s_pl", f"2n2_{j-1}_{NCH-1}")
                P.op("s_act", f"2bce2_{j}", lambda: a.copy(C["bcrs"][:, 0:T2], C["T1b"][:, 0:T2]))
                P.wait(a, "s_pe", f"2bc3_{j}")
                if j >= 1:
                    P.wait(a, "s_pl", f"2sc2_{j-1}_{NCH-1}")
                P.op("s_act", f"2bce3_{j}", lambda: a.copy(C["bcr12"][:, 0:T2], C["T1b"][:, 0:T2]))

            def h_tail(j):
                for c in range(NCH):
                    P.wait(a, "s_pl", f"2sc2_{j}_{c}")
                    if c == 0 and j >= 1:
                        P.wmil(a, f"2hst_{j-1}")
                    P.op("s_act", f"2h_{j}_{c}",
                         lambda c=c: a.activation(C["h_sb"][:, c, 0:T2], C["scr2"][:, c % 2, 0:T2], AF.Identity,
                                                  bias=M["cols"][:, 24 + c:25 + c], scale=M["cols"][:, 16 + c:17 + c]))

            emit_eq(0, 0, NCH)
            for i in range(NT2):
                P.wait(a, "s_pe", f"2zp_{i}")
                P.op("s_act", "@", lambda: a.activation(C["zr"][:, 0:T2], C["B0"][0:16, 0:T2], AF.Ln, bias=M["cols"][0:16, 72:73]))
                P.selfwait(a, "s_act")
                P.op("s_act", f"2z_{i}", lambda: a.activation(C["zr"][:, 0:T2], C["zr"][:, 0:T2], AF.Exp, scale=-1.0))
                for c in range(NCH):
                    cb = c % 2
                    P.wait(a, "s_pe", f"2zb_{i}_{c}")
                    if c >= 2:
                        P.wait(a, "s_dve", f"2at_{i}_{c-2}")
                    elif i >= 1:
                        P.wait(a, "s_dve", f"2at_{i-1}_{c + NCH - 2}")
                    P.op("s_act", f"2zbe_{i}_{c}",
                         lambda cb=cb: a.copy(C["zbc"][:, cb, 0:T2], C[f"B{cb}"][:, 0:T2]))
                if i >= 1:
                    ln2_tail(i - 1)
                if i + 1 < NT2:
                    emit_eq(i + 1, 0, 3)
                if i >= 1:
                    bce_tail(i - 1)
                if i + 1 < NT2:
                    emit_eq(i + 1, 3, NCH)
                if i >= 1:
                    h_tail(i - 1)
            ln2_tail(NT2 - 1)
            bce_tail(NT2 - 1)
            h_tail(NT2 - 1)

        def body_pe(t):
            def emit_q(it, j0=0, j1=NCH):
                if j0 == 0:
                    P.wmil(t, f"2xx_{it}")
                    P.wmil(t, f"2x_{it}")
                for j in range(j0, j1):
                    jb = j % 3
                    if j >= 3:
                        P.wait(t, "s_dve", f"2qe_{it}_{j-3}")
                    elif it >= 1:
                        P.wait(t, "s_dve", f"2qe_{it-1}_{NCH-3+j}")
                    for c in range(NCH):
                        P.op("s_pe", f"2qp_{it}_{j}" if c == NCH - 1 else None,
                             lambda j=j, jb=jb, c=c, ib=it % 2: t.matmul(C[f"G{jb}"][:, 0:T2], M["wq"][:, c, 128 * j:128 * (j + 1)], M["n1s"][:, ib, c, 0:T2],
                                                                         start=(c == 0), stop=(c == NCH - 1), skip_group_check=True))

            def ln2_stats(j):
                if j >= 1:
                    P.wait(t, "s_act", f"2bce3_{j-1}")
                for c in range(NCH):
                    cb2 = c % 2
                    P.wait(t, "s_pl", f"2xc_{j}_{c}")
                    P.op("s_pe", f"2s1_{j}_{c}",
                         lambda c=c, cb2=cb2: t.matmul(C["T1b"][:, 0:T2], M["onesm"][:, :], C["x1b"][:, cb2, 0:T2],
                                              start=(c == 0), stop=(c == NCH - 1), skip_group_check=True))
                P.wait(t, "s_act", f"2bce1_{j}")
                for c in range(NCH):
                    cb2 = c % 2
                    P.wait(t, "s_act", f"2sqc_{j}_{c}")
                    P.op("s_pe", f"2s2_{j}_{c}",
                         lambda c=c, cb2=cb2: t.matmul(C["T1b"][:, 0:T2], M["onesm"][:, :], C["sq2"][:, cb2, 0:T2],
                                                       start=(c == 0), stop=(c == NCH - 1), skip_group_check=True))

            def ln2_bc(j):
                P.wait(t, "s_act", f"2rstd1_{j}")
                P.op("s_pe", f"2bc2_{j}", lambda: t.matmul(C["T1b"][:, 0:T2], M["onesf"][:, :], M["rows"][0:1, 3, 0:T2],
                                                           start=True, stop=True, skip_group_check=True))
                P.wait(t, "s_act", f"2bce2_{j}")
                P.wait(t, "s_dve", f"2r12_{j}")
                P.op("s_pe", f"2bc3_{j}", lambda: t.matmul(C["T1b"][:, 0:T2], M["onesf"][:, :], M["rows"][0:1, 0, 0:T2],
                                                           start=True, stop=True, skip_group_check=True))

            emit_q(0)
            P.wait(t, "s_dve", "5done")
            for i in range(NT2):
                P.wait(t, "s_dve", f"2qe_{i}_{NCH-1}")
                if i >= 1:
                    P.wait(t, "s_act", f"2zbe_{i-1}_{NCH-2}")
                for c in range(NCH):
                    P.op("s_pe", f"2zp_{i}" if c == NCH - 1 else None,
                         lambda c=c: t.matmul(C["B0"][:, 0:T2], M["kmask"][:, c, :], C["q_b"][:, c, 0:T2],
                                              start=(c == 0), stop=(c == NCH - 1), skip_group_check=True))
                P.wait(t, "s_act", f"2z_{i}")
                for c in range(NCH):
                    cb = c % 2
                    if c >= 2:
                        P.wait(t, "s_act", f"2zbe_{i}_{c-2}")
                    elif i >= 1:
                        P.wait(t, "s_act", f"2zbe_{i-1}_{c + NCH - 2}")
                    P.op("s_pe", f"2zb_{i}_{c}",
                         lambda c=c, cb=cb: t.matmul(C[f"B{cb}"][:, 0:T2], M["e2"][:, c, :], C["zr"][:, 0:T2],
                                                     start=True, stop=True, skip_group_check=True))
                    if c >= 2:
                        P.wait(t, "s_dve", f"2at_{i}_{c-2}")
                    elif i >= 1:
                        P.wait(t, "s_dve", f"2x1_{i-1}_{NCH-2+c}")
                    P.op("s_pe", f"2ap_{i}_{c}",
                         lambda c=c, cb=cb: t.matmul(C[f"A{cb}"][:, 0:T2], M["kvt"][:, c, :], C["q_b"][:, c, 0:T2],
                                                     start=True, stop=True, skip_group_check=True))
                if i >= 1:
                    ln2_stats(i - 1)
                if i + 1 < NT2:
                    emit_q(i + 1, 0, 5)
                if i >= 1:
                    ln2_bc(i - 1)
                if i + 1 < NT2:
                    emit_q(i + 1, 5, NCH)
                P.wait(t, "s_dve", f"2at_{i}_{NCH-1}")
                for k in range(NCH):
                    kb = k % 2
                    if k >= 2:
                        P.wait(t, "s_dve", f"2x1_{i}_{k-2}")
                    else:
                        P.wait(t, "s_dve", f"2at_{i}_{NCH-2+k}")
                    for c in range(NCH):
                        P.op("s_pe", f"2wo_{i}_{k}" if c == NCH - 1 else None,
                             lambda k=k, kb=kb, c=c: t.matmul(C[f"A{kb}"][:, 0:T2], M["wo"][:, c, 128 * k:128 * (k + 1)], C["at_b"][:, c, 0:T2],
                                                              start=(c == 0), stop=(c == NCH - 1), skip_group_check=True))
            ln2_stats(NT2 - 1)
            ln2_bc(NT2 - 1)

        def body_dve(v):
            P.wmil(v, "5ws_62")
            P.wmil(v, "5ws_63")
            P.op("s_dve", "@", lambda: v.tensor_scalar(M["dcol"][:, 0:8], M["cols"][:, 0:8], 1.0, None, ALU.add))
            P.op("s_dve", "5dcol", lambda: v.tensor_scalar(M["dcol"][:, 8:16], M["cols"][:, 32:40], 0.5, None, ALU.mult))
            P.wmil(v, "5ccl")
            P.op("s_dve", "@", lambda: v.memset(M["kvt"][:, :, :], 0.0))
            P.op("s_dve", "@", lambda: v.memset(M["kmask"][:, :, :], 0.0))
            P.selfwait(v, "s_dve")
            for j in range(NCH):
                P.op("s_dve", None, lambda j=j: v.tensor_copy(M["kvt"][0:64, j, 0:64], ccs_kv[0:64, j, 0:64]))
                P.op("s_dve", None, lambda j=j: v.tensor_copy(M["kvt"][64:128, j, 64:128], ccs_kv[64:128, j, 64:128]))
                P.op("s_dve", None, lambda j=j: v.tensor_copy(M["kmask"][0:64, j, 2 * j:2 * j + 1], M["ccs"][0:64, 1024 + j:1025 + j]))
                ms = "5done" if j == NCH - 1 else None
                P.op("s_dve", ms, lambda j=j: v.tensor_copy(M["kmask"][64:128, j, 2 * j + 1:2 * j + 2], M["ccs"][64:128, 1024 + j:1025 + j]))

            def emit_qmqe(it):
                def emit_qm(j):
                    jb = j % 3
                    P.wait(v, "s_pe", f"2qp_{it}_{j}")
                    P.op("s_dve", f"2qm_{it}_{j}",
                         lambda j=j, jb=jb: v.tensor_scalar(C["e_q"][:, jb, 0:T2], C[f"G{jb}"][:, 0:T2], M["cols"][:, j:j + 1], 0.0, ALU.add, ALU.min))

                def emit_qe(j):
                    jb = j % 3
                    P.wait(v, "s_act", f"2eq_{it}_{j}")
                    P.op("s_dve", f"2qe_{it}_{j}",
                         lambda j=j, jb=jb: v.scalar_tensor_tensor(C["q_b"][:, j, 0:T2], C[f"G{jb}"][:, 0:T2], M["dcol"][:, j:j + 1], C["e_q"][:, jb, 0:T2], ALU.add, ALU.max))

                if it >= 1:
                    P.wait(v, "s_pe", f"2zp_{it-1}")
                    P.wait(v, "s_pe", f"2ap_{it-1}_{NCH-1}")
                emit_qm(0)
                for j in range(1, NCH):
                    emit_qm(j)
                    emit_qe(j - 1)
                emit_qe(NCH - 1)

            def rowmath(j):
                if j >= 1:
                    P.wait(v, "s_pe", f"2bc3_{j-1}")
                P.wait(v, "s_act", f"2bce1_{j}")
                P.op("s_dve", "@", lambda: v.tensor_tensor(M["rows"][0:1, 2, 0:T2], C["bcmu"][0:1, 0:T2], C["bcmu"][0:1, 0:T2], ALU.mult))
                P.wait(v, "s_pe", f"2s2_{j}_{NCH-1}")
                P.selfwait(v, "s_dve")
                P.op("s_dve", "@", lambda: v.tensor_tensor(M["rows"][0:1, 1, 0:T2], C["T1b"][0:1, 0:T2], M["rows"][0:1, 2, 0:T2], ALU.subtract))
                P.selfwait(v, "s_dve")
                P.op("s_dve", f"2rowA_{j}", lambda: v.tensor_scalar(M["rows"][0:1, 4, 0:T2], M["rows"][0:1, 1, 0:T2], EPS, None, ALU.add))
                P.wait(v, "s_act", f"2rw_{j}")
                P.op("s_dve", "@", lambda: v.tensor_tensor(M["rows"][0:1, 1, 0:T2], M["rows"][0:1, 1, 0:T2], M["rows"][0:1, 2, 0:T2], ALU.mult))
                P.selfwait(v, "s_dve")
                P.op("s_dve", f"2rowB_{j}", lambda: v.tensor_scalar(M["rows"][0:1, 1, 0:T2], M["rows"][0:1, 1, 0:T2], EPS, None, ALU.add))
                P.wait(v, "s_act", f"2rstd2_{j}")
                P.op("s_dve", f"2r12_{j}", lambda: v.tensor_tensor(M["rows"][0:1, 0, 0:T2], M["rows"][0:1, 3, 0:T2], M["rows"][0:1, 4, 0:T2], ALU.mult))

            def cprime_k(j, k):
                P.wait(v, "s_pl", f"2n2_{j}_{k}")
                if k == 0 and j >= 1:
                    P.wmil(v, f"2cst_{j-1}")
                P.op("s_dve", f"2c_{j}_{k}",
                     lambda k=k: v.scalar_tensor_tensor(C["cp"][:, k, 0:T2], C["scr1"][:, k % 2, 0:T2], M["dcol"][:, 8 + k:9 + k], C["x1"][:, k, 0:T2], ALU.add, ALU.add))

            emit_qmqe(0)
            for i in range(NT2):
                ib = i % 2
                for c in range(NCH):
                    cb = c % 2
                    P.wait(v, "s_pe", f"2ap_{i}_{c}")
                    P.wait(v, "s_act", f"2zbe_{i}_{c}")
                    P.op("s_dve", f"2at_{i}_{c}",
                         lambda c=c, cb=cb: v.tensor_tensor(C["at_b"][:, c, 0:T2], C[f"A{cb}"][:, 0:T2], C["zbc"][:, cb, 0:T2], ALU.mult))
                if i >= 1:
                    rowmath(i - 1)
                if i + 1 < NT2:
                    emit_qmqe(i + 1)
                for k in range(NCH):
                    kb = k % 2
                    if i >= 1:
                        cprime_k(i - 1, k)
                        P.selfwait(v, "s_dve")
                    P.wait(v, "s_pe", f"2wo_{i}_{k}")
                    if k == 0 and i >= 1:
                        P.wait(v, "s_act", f"2sqc_{i-1}_{NCH-1}")
                        P.wait(v, "s_pe", f"2s1_{i-1}_{NCH-1}")
                        P.wait(v, "s_pl", f"2t_{i-1}")
                    P.op("s_dve", f"2x1_{i}_{k}",
                         lambda k=k, kb=kb, ib=ib: v.scalar_tensor_tensor(C["x1"][:, k, 0:T2], C[f"A{kb}"][:, 0:T2], M["cols"][:, 8 + k:9 + k], M["xts"][:, ib, k, 0:T2], ALU.add, ALU.add))
            rowmath(NT2 - 1)
            for k in range(NCH):
                cprime_k(NT2 - 1, k)

        def body_gp(g):
            P.wait(g, "s_cc", "5cc")
            P.dma(g, "dcc", "5ccl", lambda: g.dma_start(out=M["ccs"][:, :], in_=ccout.ap()))

            for i in range(NT2):
                sl = slice(i * T2, (i + 1) * T2)
                for c in range(NCH):
                    cb2 = c % 2
                    P.wait(g, "s_dve", f"2x1_{i}_{c}")
                    if c >= 2:
                        P.wait(g, "s_pe", f"2s1_{i}_{c-2}")
                    elif i >= 1:
                        P.wait(g, "s_pe", f"2s1_{i-1}_{c + NCH - 2}")
                    P.op("s_pl", f"2xc_{i}_{c}",
                         lambda c=c, cb2=cb2: g.tensor_copy(C["x1b"][:, cb2, 0:T2], C["x1"][:, c, 0:T2]))
                P.wait(g, "s_act", f"2bce1_{i}")
                P.wait(g, "s_dve", f"2x1_{i}_{NCH-1}")
                for c in range(NCH):
                    P.op("s_pl", f"2t_{i}" if c == NCH - 1 else "@",
                         lambda c=c: g.tensor_tensor(M["t0"][:, c, 0:T2], C["x1"][:, c, 0:T2], C["bcmu"][:, 0:T2], ALU.subtract))
                P.selfwait(g, "s_pl")
                P.wait(g, "s_act", f"2bce3_{i}")
                for k in range(NCH):
                    if k >= 2:
                        P.wait(g, "s_act", f"2h_{i}_{k-2}")
                    elif i >= 1:
                        P.wait(g, "s_act", f"2h_{i-1}_{NCH-2+k}")
                    P.op("s_pl", f"2sc2_{i}_{k}",
                         lambda k=k: g.tensor_tensor(C["scr2"][:, k % 2, 0:T2], M["t0"][:, k, 0:T2], C["bcr12"][:, 0:T2], ALU.mult))
                    if k >= 2:
                        P.wait(g, "s_dve", f"2c_{i}_{k-2}")
                    P.op("s_pl", f"2n2_{i}_{k}",
                         lambda k=k: g.tensor_tensor(C["scr1"][:, k % 2, 0:T2], M["t0"][:, k, 0:T2], C["bcrs"][:, 0:T2], ALU.mult))
                P.wait(g, "s_act", f"2h_{i}_{NCH-1}")
                P.dma(g, "dhs", f"2hst_{i}", lambda sl=sl: g.dma_start(out=hd3[:, :, sl], in_=C["h_sb"][:, :, 0:T2]))
                P.wait(g, "s_dve", f"2c_{i}_{NCH-1}")
                P.dma(g, "dcs", f"2cst_{i}", lambda sl=sl: g.dma_start(out=out3[:, :, sl], in_=C["cp"][:, :, 0:T2]))

        if P.real:
            with nc.Block("p2a") as blk:
                blk.sync(body_sync)
                blk.scalar(body_act)
                blk.tensor(body_pe)
                blk.vector(body_dve)
                blk.gpsimd(body_gp)
        else:
            body_sync(None); body_act(None); body_pe(None); body_dve(None); body_gp(None)


    def emit_p2b(Dd):
        def body_sync(sy):
            P.wmil(sy, "5ws_62")
            P.wmil(sy, "5ws_63")
            P.wmil(sy, f"2hst_{NT2-1}")
            P.wmil(sy, f"2cst_{NT2-1}")
            for i in (0, 1):
                P.dma(sy, "dh", f"3h_{i}",
                      lambda ib=i % 2, sl=slice(i * T3, (i + 1) * T3):
                      sy.dma_start(out=Dd["h_t"][:, ib, :, 0:T3], in_=hd3[:, :, sl]))
            w2r = w2d.rearrange("c p n -> p c n")
            w2sems = ["dn", "dx", "dq", "dq2"]
            for q in range(4):
                P.dma(sy, w2sems[q], f"3w2_{q}",
                      lambda q=q: sy.dma_start(out=Dd["w2b"][:, :, 256 * q:256 * (q + 1)],
                                               in_=w2r[:, :, 256 * q:256 * (q + 1)]))
            for i in range(2, NT3):
                ib = i % 2
                sl = slice(i * T3, (i + 1) * T3)
                P.wait(sy, "s_pe", f"3y_{i-2}_{NFC-1}")
                P.dma(sy, "dh", f"3h_{i}", lambda ib=ib, sl=sl: sy.dma_start(out=Dd["h_t"][:, ib, :, 0:T3], in_=hd3[:, :, sl]))

        def body_pe(t):
            for i in range(NT3):
                ib = i % 2
                P.wmil(t, f"3h_{i}")
                for j in range(NFC):
                    jb = j % 2
                    if i == 0 and j == 0:
                        for q in range(NCH):
                            P.wmil(t, f"3w1_{q}")
                    if j >= 2:
                        P.wait(t, "s_act", f"3sl_{i}_{j-2}")
                    elif i >= 1:
                        P.wait(t, "s_act", f"3sl_{i-1}_{NFC-2+j}")
                    for c in range(NCH):
                        P.op("s_pe", f"3y_{i}_{j}" if c == NCH - 1 else None,
                             lambda j=j, jb=jb, c=c, ib=ib: t.matmul(Dd[f"yp{jb}"][:, 0:T3], Dd["w1b"][:, c, 128 * j:128 * (j + 1)], Dd["h_t"][:, ib, c, 0:T3],
                                                                     start=(c == 0), stop=(c == NCH - 1), skip_group_check=True))
                for k in range(NCH):
                    kb = k % 2
                    if i == 0 and k % 2 == 0:
                        P.wmil(t, f"3w2_{k // 2}")
                    if k == 0:
                        P.wait(t, "s_act", f"3sl_{i}_{NFC-1}")
                    if k >= 2:
                        P.wait(t, "s_act", f"3z_{i}_{k-2}")
                    elif i >= 1:
                        P.wait(t, "s_act", f"3z_{i-1}_{NCH-2+k}")
                    for j in range(NFC):
                        P.op("s_pe", f"3f2_{i}_{k}" if j == NFC - 1 else None,
                             lambda j=j, k=k, kb=kb: t.matmul(Dd[f"zp{kb}"][:, 0:T3], Dd["w2b"][:, j, 128 * k:128 * (k + 1)], Dd["y_sb"][:, j, 0:T3],
                                                              start=(j == 0), stop=(j == NFC - 1), skip_group_check=True))

        def body_act(a):
            P.wmil(a, f"2hst_{NT2-1}")
            P.wmil(a, f"2cst_{NT2-1}")
            for i in range(NT3):
                ib = i % 2
                for j in range(NFC):
                    jb = j % 2
                    P.wait(a, "s_pe", f"3y_{i}_{j}")
                    if j == 0 and i >= 1:
                        P.wait(a, "s_pe", f"3f2_{i-1}_{NCH-1}")
                    P.op("s_act", f"3sl_{i}_{j}",
                         lambda j=j, jb=jb: a.activation(Dd["y_sb"][:, j, 0:T3], Dd[f"yp{jb}"][:, 0:T3], AF.Silu, bias=M0["b1col"][:, j:j + 1]))
                for k in range(NCH):
                    kb = k % 2
                    hb = 0 if k < 4 else 1
                    P.wait(a, "s_pe", f"3f2_{i}_{k}")
                    if i >= 1 and k == 0:
                        P.wmil(a, f"3ostA_{i-1}")
                    if i >= 1 and k == 4:
                        P.wmil(a, f"3ostB_{i-1}")
                    P.op("s_act", f"3z_{i}_{k}",
                         lambda k=k, kb=kb, hb=hb: a.mul(Dd["outst"][:, hb, k % 4, 0:T3], Dd[f"zp{kb}"][:, 0:T3], 0.5))

        def body_gp(g):
            P.wmil(g, f"2cst_{NT2-1}")
            for i in range(NT3):
                ib = i % 2
                sl = slice(i * T3, (i + 1) * T3)
                P.wait(g, "s_act", f"3z_{i}_{3}")
                P.dma(g, "dos", f"3ostA_{i}",
                      lambda sl=sl: g.dma_start(out=out3[:, 0:4, sl], in_=Dd["outst"][:, 0, :, 0:T3], accum_op=ALU.add))
                P.wait(g, "s_act", f"3z_{i}_{NCH-1}")
                P.dma(g, "dos", f"3ostB_{i}",
                      lambda sl=sl: g.dma_start(out=out3[:, 4:8, sl], in_=Dd["outst"][:, 1, :, 0:T3], accum_op=ALU.add))
            P.selfwait(g, "dos")
            P.selfwait(g, "dcs")
            P.selfwait(g, "dhs")

        if P.real:
            with nc.Block("p2b") as blk:
                blk.sync(body_sync)
                blk.tensor(body_pe)
                blk.scalar(body_act)
                blk.gpsimd(body_gp)
        else:
            body_sync(None); body_pe(None); body_act(None); body_gp(None)

    from contextlib import ExitStack
    M0 = {}

    def sb(es, name, shape, dt):
        return es.enter_context(nc.sbuf_tensor(name, shape, dt))

    def ps(es, name, shape):
        return es.enter_context(nc.psum_tensor(name, shape, F32))

    with ExitStack() as es0:
        P.sems = {n: es0.enter_context(nc.semaphore(n))
                  for n in ["dp0", "dx", "dn", "dh", "dw1", "dw2", "dn1s",
                            "dq", "dq2", "dq3", "dq4", "ds", "ds2", "ds3", "ds4", "dcc", "dws", "dws2", "dhs", "dcs", "dos",
                            "s_cc", "s_pe", "s_dve", "s_act", "s_pl"]}
        b1col_t = sb(es0, "b1col", [128, 32], F32)
        M0["b1col"] = b1col_t

        P.reset(False)
        Mn = {"b1col": None}
        emit_p1(Mn, None)
        emit_p2a(Mn, None, None)
        emit_p2b(None)

        P.reset(True)
        TT = max(T1, T2)
        with ExitStack() as esA:
            arena = sb(esA, "arena", [128, 32768], BF16)
            arena_f = arena.bitcast(F32)
            wq_v = arena.ap()[:, 0:NCH * D].rearrange("p (c n) -> p c n", n=D)
            wo_v = arena.ap()[:, NCH * D:2 * NCH * D].rearrange("p (c n) -> p c n", n=D)
            xts_v = arena_f.ap()[:, NCH * D:NCH * D + 2 * NCH * TT].rearrange(
                "p (b c t) -> p b c t", b=2, c=NCH)
            w1b_v = arena.ap().rearrange("p (c n) -> p c n", n=FF)
            assert NCH * D + 2 * NCH * TT <= 16384
            es1 = esA.enter_context(ExitStack())
            M = dict(
                cols=sb(es1, "cols", [128, 73], F32),
                dcol=sb(es1, "dcol", [128, 16], F32),
                kvt=sb(es1, "kvt", [128, NCH, 128], BF16),
                kmask=sb(es1, "kmask", [128, NCH, 128], BF16),
                e2=sb(es1, "e2", [16, NCH, 128], F32),
                ccs=sb(es1, "ccs", [128, 1032], F32),
                onesf=sb(es1, "onesf_s", [1, 128], F32),
                onesb=sb(es1, "onesb_s", [1, 128], BF16),
                onescb=sb(es1, "onescb_s", [128, 1], BF16),
                onesm=sb(es1, "onesm_s", [128, 128], BF16),
                onesmf=sb(es1, "onesmf_s", [128, 128], F32),
                bkb=sb(es1, "bkb", [1, D], BF16),
                bvb=sb(es1, "bvb", [1, D], BF16),
                rows=sb(es1, "rows", [1, 5, max(T1, T2)], F32),
                wq=wq_v,
                wo=wo_v,
                xts=xts_v,
                n1s=sb(es1, "n1s", [128, 2, NCH, TT], BF16),
                t0=sb(es1, "t0", [128, NCH, TT], F32),
                b1col=b1col_t,
            )
            with ExitStack() as es2o:
                kv_ps = dict(
                    KVp=ps(es2o, "KVp", [128, NCH, 128]),
                    ksump=ps(es2o, "ksump", [128, NCH]),
                )
                with ExitStack() as es2:
                    A = dict(
                        wk=sb(es2, "wk", [128, NCH, D], BF16),
                        wv=sb(es2, "wv", [128, NCH, D], BF16),
                        sq1=sb(es2, "sq1", [128, NCH, T1], BF16),
                        k_sb=sb(es2, "k_sb", [128, 2, D], BF16),
                        v_sb=sb(es2, "v_sb", [128, 2, D], BF16),
                        e=sb(es2, "e", [128, 2, D], BF16),
                        kp=ps(es2, "kp", [128, 1024]),
                        vp=ps(es2, "vp", [128, 1024]),
                        T1p=ps(es2, "T1p", [128, 512]),
                        mub=sb(es2, "mub", [128, T1], F32),
                        qbf=sb(es2, "qbf", [128, 4, 1024], I32),
                        sbc=sb(es2, "sbc", [128, 2, 1024], F32),
                        wch=sb(es2, "wch", [128, 2, 1024], BF16),
                        **kv_ps,
                    )
                    emit_p1(M, A)
            with ExitStack() as es4:
                C = dict(
                    e_q=sb(es4, "e_q", [128, 3, T2], BF16),
                    x1b=sb(es4, "x1b", [128, 2, T2], BF16),
                    q_b=sb(es4, "q_b", [128, NCH, T2], BF16),
                    at_b=sb(es4, "at_b", [128, NCH, T2], BF16),
                    zbc=sb(es4, "zbc", [128, 2, T2], F32),
                    zr=sb(es4, "zr", [16, T2], F32),
                    x1=sb(es4, "x1", [128, NCH, T2], F32),
                    sq2=sb(es4, "sq2", [128, 2, T2], BF16),
                    bcmu=sb(es4, "bcmu", [128, T2], F32),
                    bcrs=sb(es4, "bcrs", [128, T2], F32),
                    bcr12=sb(es4, "bcr12", [128, T2], F32),
                    scr1=sb(es4, "scr1", [128, 2, T2], F32),
                    scr2=sb(es4, "scr2", [128, 2, T2], F32),
                    h_sb=sb(es4, "h_sb", [128, NCH, T2], BF16),
                    cp=sb(es4, "cp", [128, NCH, T2], F32),
                    B0=ps(es4, "B0", [128, T2]),
                    B1=ps(es4, "B1", [128, T2]),
                    T1b=ps(es4, "T1b", [128, T2]),
                    G0=ps(es4, "G0", [128, T2]),
                    G1=ps(es4, "G1", [128, T2]),
                    G2=ps(es4, "G2", [128, T2]),
                    A0=ps(es4, "A0", [128, T2]),
                    A1=ps(es4, "A1", [128, T2]),
                )
                emit_p2a(M, C, kv_ps)
            es1.close()
            es5 = esA.enter_context(ExitStack())
            Dd = dict(
                w1b=w1b_v,
                w2b=sb(es5, "w2b", [128, NFC, D], BF16),
                h_t=sb(es5, "h_t", [128, 2, NCH, T3], BF16),
                y_sb=sb(es5, "y_sb", [128, NFC, T3], BF16),
                outst=sb(es5, "outst", [128, 2, 4, T3], F32),
                yp0=ps(es5, "yp0", [128, T3]),
                yp1=ps(es5, "yp1", [128, T3]),
                zp0=ps(es5, "zp0", [128, T3]),
                zp1=ps(es5, "zp1", [128, T3]),
            )
            emit_p2b(Dd)

    return nc


_NC_CACHE = {}


def _get_nc():
    key = tuple(sorted(CFG.items()))
    if key not in _NC_CACHE:
        _NC_CACHE[key] = build_nc(CFG)
    return _NC_CACHE[key]


def prep_inputs(inputs, cfg):
    TPC = cfg["TPC"]
    ncores = cfg["NCORES"]
    x = np.asarray(inputs["L0_attn_hidden_states"], dtype=np.float32)
    ntok = x.shape[0] * x.shape[1]
    assert ntok == TPC * ncores
    xall = np.ascontiguousarray(x.reshape(ntok, D).T)

    def T(a):
        return np.ascontiguousarray(np.asarray(a, dtype=np.float32).T)

    def colform(a, nch):
        return np.ascontiguousarray(np.asarray(a, dtype=np.float32).reshape(nch, 128).T)

    wqt = T(inputs["L0_attn_Wq"])
    wkt = T(inputs["L0_attn_Wk"])
    wvt = T(inputs["L0_attn_Wv"])
    wot = T(inputs["L0_attn_Wo"])
    fc1q = np.ascontiguousarray(np.asarray(inputs["L0_ffn_fc1_q"], dtype=np.int32))
    fc2q = np.ascontiguousarray(np.asarray(inputs["L0_ffn_fc2_q"], dtype=np.int32))
    fc1s = np.ascontiguousarray(np.asarray(inputs["L0_ffn_fc1_scales"], dtype=np.float32))
    fc2s = np.ascontiguousarray(np.asarray(inputs["L0_ffn_fc2_scales"], dtype=np.float32))
    bkrow = np.asarray(inputs["L0_attn_bk"], dtype=np.float32).reshape(1, D).copy()
    bvrow = np.asarray(inputs["L0_attn_bv"], dtype=np.float32).reshape(1, D).copy()
    colpack = np.zeros((128, 73), np.float32)
    colpack[:, 72] = 1e-6
    colpack[:, 0:8] = colform(inputs["L0_attn_bq"], NCH)
    colpack[:, 8:16] = colform(inputs["L0_attn_bo"], NCH)
    colpack[:, 16:24] = colform(inputs["L0_ffn_ln_gamma"], NCH)
    colpack[:, 24:32] = colform(inputs["L0_ffn_ln_beta"], NCH)
    colpack[:, 32:40] = colform(inputs["L0_ffn_fc2_bias"], NCH)
    b1c = colform(inputs["L0_ffn_fc1_bias"], NFC)
    onesf = np.ones((1, 128), np.float32)
    onesmin = np.full((128, 128), 1.0 / 1024.0, ml_dtypes.bfloat16)
    onesmf = np.full((128, 128), 1.0 / 1024.0, np.float32)
    onesb = np.ones((1, 128), ml_dtypes.bfloat16)
    onescb = np.ones((128, 1), ml_dtypes.bfloat16)
    e2 = np.zeros((16, NCH, 128), np.float32)
    for c in range(NCH):
        for a in range(2):
            e2[2 * c + a, c, 64 * a:64 * (a + 1)] = 1.0
    e2 = e2.reshape(16, NCH * 128)

    shared = dict(wqt=wqt, wkt=wkt, wvt=wvt, wot=wot, fc1q=fc1q, fc2q=fc2q,
                  fc1s=fc1s, fc2s=fc2s, bkrow=bkrow, bvrow=bvrow,
                  colpack=colpack, b1c=b1c, onesf=onesf, onesb=onesb,
                  onescb=onescb, onesmin=onesmin, onesmf=onesmf, e2in=e2)
    in_maps = []
    for c in range(ncores):
        m = dict(shared)
        m["xt"] = np.ascontiguousarray(xall[:, c * TPC:(c + 1) * TPC])
        in_maps.append(m)
    return in_maps


def assemble_output(results, x_shape, cfg):
    TPC = cfg["TPC"]
    ncores = cfg["NCORES"]
    cols = [np.asarray(results[c]["out"], dtype=np.float32) for c in range(ncores)]
    full = np.concatenate(cols, axis=1)
    return np.ascontiguousarray(full.T).reshape(x_shape)


def kernel(**inputs):
    cfg = CFG
    nc = _get_nc()
    in_maps = prep_inputs(inputs, cfg)
    res = run_bass_kernel_spmd(nc, in_maps, core_ids=list(range(cfg["NCORES"])))
    x_shape = np.asarray(inputs["L0_attn_hidden_states"]).shape
    return assemble_output(res.results, x_shape, cfg)
